# version 17
# baseline (speedup 1.0000x reference)
import numpy as np
import ml_dtypes
from contextlib import ExitStack

import concourse.bass as bass
import concourse.mybir as mybir
from concourse.bass_utils import run_bass_kernel_spmd

F32 = mybir.dt.float32
BF16 = mybir.dt.bfloat16
AF = mybir.ActivationFunctionType
ALU = mybir.AluOpType
AX = mybir.AxisListType

NC = 8
D = 2048
DK = 16
E = 8192
ECH = 64
T = 512
NT = 4
SC = NT * T

ENGS = ("sync", "scalar", "vector", "tensor", "gpsimd")


class Ctr:
    def __init__(self, h):
        self.h = h
        self.n = 0

    def bump(self, k=1):
        self.n += k
        return self.n


class Prog:

    def __init__(self, nc, es):
        self.nc = nc
        self.es = es
        self.q = {e: [] for e in ENGS}
        self._last_wait = {}
        self.chains = {}

    def sem(self, name):
        return Ctr(self.es.enter_context(self.nc.semaphore(name)))

    def init_chains(self):
        for e in ("vector", "scalar", "gpsimd"):
            self.chains[e] = self.sem("chain_" + e)

    def emit(self, eng, fn, wait=(), inc=None, inc_k=1, dma=False):
        chain = self.chains.get(eng)
        wait = list(wait)
        if chain is not None:
            wait.append((chain, chain.n))
        waits = []
        for c, v in wait:
            if v <= 0:
                continue
            key = (eng, id(c.h))
            if self._last_wait.get(key, -1) < v:
                self._last_wait[key] = v
                waits.append((c.h, v))
        if inc is chain:
            inc = None
        post = inc.bump(inc_k) if inc is not None else None
        chain_post = None
        if chain is not None and not dma:
            chain_post = chain.bump(1)

        def run(e, fn=fn, waits=waits, inc=inc, inc_k=inc_k,
                chain=chain, chain_post=chain_post):
            for h, v in waits:
                e.wait_ge(h, v)
            r = fn(e)
            if inc is not None:
                r.then_inc(inc.h, inc_k)
            if chain_post is not None:
                r.then_inc(chain.h, 1)
        self.q[eng].append(run)
        return chain_post if chain_post is not None else post

    def barrier_wait(self, eng, *pairs):
        pairs = [(c, v) for c, v in pairs if v > 0]
        if pairs:
            self.emit(eng, lambda e: None, wait=pairs, dma=True)

    def replay(self):
        with self.nc.Block() as block:
            for eng in ENGS:
                ops = self.q[eng]
                if not ops:
                    continue
                getattr(block, eng)(lambda e, ops=ops: [f(e) for f in ops])


def _sb(nc, es, name, shape, dtype):
    return es.enter_context(nc.sbuf_tensor(name, shape, dtype))


def _ps(nc, es, name, shape, dtype=F32):
    return es.enter_context(nc.psum_tensor(name, shape, dtype))


def build_fused():
    nc = bass.Bass("TRN2", target_bir_lowering=False)
    I32 = mybir.dt.int32
    x_in = nc.dram_tensor("x", [SC, D], F32, kind="ExternalInput").ap()
    kvp_in = nc.dram_tensor("kvp", [32, 128, DK * 128], F32, kind="ExternalInput").ap()
    bkv_in = nc.dram_tensor("bkv", [128, 32], F32, kind="ExternalInput").ap()
    qp_in = nc.dram_tensor("qp", [DK, 128, DK * 128], F32, kind="ExternalInput").ap()
    wo_in = nc.dram_tensor("wop", [DK, 128, DK * 128], F32, kind="ExternalInput").ap()
    bqo_in = nc.dram_tensor("bqo", [128, 32], F32, kind="ExternalInput").ap()
    cos_in = nc.dram_tensor("cost", [512, SC], BF16, kind="ExternalInput").ap()
    sin_in = nc.dram_tensor("sint", [512, SC], BF16, kind="ExternalInput").ap()
    scs_in = nc.dram_tensor("scs", [24, 128, 2048], F32, kind="ExternalInput").ap()
    qsrc_in = nc.dram_tensor("qsrc", [96, 128, 2048], F32, kind="ExternalInput").ap()
    slot_in = nc.dram_tensor("slotv", [1, 1], I32, kind="ExternalInput").ap()
    out_d = nc.dram_tensor("out", [SC, D], F32, kind="ExternalOutput").ap()

    xn_dram = nc.dram_tensor("xn_blob", [128, DK, SC], BF16).ap()
    kvb_bf = nc.dram_tensor("kvb_bf", [32, 128, DK * 128], BF16).ap()
    qwb_bf = nc.dram_tensor("qwb_bf", [32, 128, DK * 128], BF16).ap()
    blob_ug = nc.dram_tensor("blob_ug", [128, 2 * ECH * 2048], BF16,
                             addr_space="Shared").ap()
    blob_dn = nc.dram_tensor("blob_dn", [128, DK * 8192], BF16,
                             addr_space="Shared").ap()
    scb_in = nc.dram_tensor("scb_in", [128, 3], F32)
    scb_out = nc.dram_tensor("scb_out", [128, 3], F32)
    ab_in = nc.dram_tensor("ab_in", [128, DK], F32)
    ab_out = nc.dram_tensor("ab_out", [128, DK], F32)
    bb_in = nc.dram_tensor("bb_in", [1, 4], F32)
    bb_out = nc.dram_tensor("bb_out", [1, 4], F32)

    id_b_t = nc.inline_tensor(np.eye(128, dtype=ml_dtypes.bfloat16), "id_b")
    id_f_t = nc.inline_tensor(np.eye(128, dtype=np.float32), "id_f")
    ones_b_t = nc.inline_tensor(np.ones((128, 1), dtype=ml_dtypes.bfloat16), "ones_b")
    ones_f_t = nc.inline_tensor(np.ones((1, 128), dtype=np.float32), "ones_f")
    ones_c_t = nc.inline_tensor(np.ones((128, 1), dtype=np.float32), "ones_c")
    eps_t = nc.inline_tensor(np.full((128, 1), 1e-5, dtype=np.float32), "eps_c")

    with ExitStack() as es:
        p = Prog(nc, es)
        KB = 1024
        base, _top = nc.bump_sbuf(212832, align_bytes=64)

        def at(name, shape, dtype, off):
            return nc.alloc_sbuf_tensor_at(name, shape, dtype, offset=base + off)
        xt1 = at("xt1", [128, 4, D], F32, 0)
        xnb = at("xnb", [128, 4, D], BF16, 32 * KB)
        ksb = at("ksb", [128, DK, T], BF16, 48 * KB)
        vsb = at("vsb", [128, DK, T], BF16, 64 * KB)
        krot = at("krot", [128, 8, T], F32, 80 * KB)
        kwb = at("kwb", [128, 2, DK * 128], BF16, 96 * KB)
        rsc1 = at("rsc1", [128, T], F32, 104 * KB)
        rsc2 = at("rsc2", [128, T], F32, 106 * KB)
        ksq = at("ksq", [128, 2, T], BF16, 108 * KB)
        invb = at("invb", [128, T], F32, 110 * KB)
        vs_scr = at("vs_scr", [128, T], F32, 112 * KB)
        pr_scr = at("pr_scr", [128, T], F32, 114 * KB)
        sqs = at("sqs", [128, D], F32, 116 * KB)
        hid = at("hid", [128, ECH, T], BF16, 0)
        upsl = at("upsl", [128, 2, 2, DK * 128], BF16, 64 * KB)
        gtsl = at("gtsl", [128, 2, 2, DK * 128], BF16, 80 * KB)
        dnsl = at("dnsl", [128, 2, ECH * 128], BF16, 96 * KB)
        q_sb = at("q_sb", [128, DK, T], F32, 0)
        qrot = at("qrot", [128, 8, T], F32, 32 * KB)
        y_sb = at("y_sb", [128, DK, T], BF16, 48 * KB)
        xt4 = at("xt4", [128, 4, D], F32, 64 * KB)
        out_sb = at("out_sb", [128, 4, D], F32, 96 * KB)
        qpb = at("qpb", [128, 2, DK * 128], BF16, 128 * KB)
        wob = at("wob", [128, 2, DK * 128], BF16, 136 * KB)
        s_scr = at("s_scr", [128, T], F32, 144 * KB)
        pr2 = at("pr2", [128, T], F32, 146 * KB)
        silu_u = at("silu_u", [128, 2, T], BF16, 148 * KB)
        xnt2 = at("xnt2", [128, DK, T], BF16, 150 * KB)
        qf = at("qf", [128, 2, 2048], F32, 124 * KB)
        m1b = at("m1b", [128, 2, 2048], BF16, 140 * KB)
        m2b = at("m2b", [128, 2, 2048], BF16, 148 * KB)
        tb = at("tb", [128, 2, 2048], BF16, 156 * KB)

        xnt = at("xnt", [128, DK, T], BF16, 166 * KB)
        g_sb = at("g_sb", [128, DK, T], BF16, 182 * KB)
        cosb = at("cosb", [128, 4, T], BF16, 198 * KB)
        sinb = at("sinb", [128, 4, T], BF16, 202 * KB)
        nrm = pr_scr
        mf = at("mf", [128, 390], F32, 206 * KB)
        stat = mf[:, 0:16]
        a_tmp = mf[:, 16:32]
        a_acc = mf[:, 32:48]
        a_sb = a_acc
        sc_acc = mf[:, 48:51]
        spp = sc_acc
        ab_c = mf[:, 51:52]
        colc = mf[:, 52:60]
        bqo_sb = mf[:, 60:92]
        bkv_sb = mf[:, 92:124]
        onc = mf[:, 124:125]
        epsc = mf[:, 125:126]
        idf = mf[:, 126:254]
        onf = mf[0:1, 254:382]
        sctot = mf[0:1, 382:385]
        pk = mf[0:1, 385:390]
        slot_sb = at("slot_sb", [1, 1], mybir.dt.int32, 206 * KB + 1568)
        mb = at("mb", [128, 129], BF16, 206 * KB + 1600)
        idb = mb[:, 0:128]
        onb = mb[:, 128:129]

        B = [_ps(nc, es, f"pb{i}", [128, T], F32) for i in range(7)]
        B7 = _ps(nc, es, "pb7", [128, T], BF16)

        p.init_chains()
        s_v = p.chains["vector"]
        s_s = p.chains["scalar"]
        s_g = p.chains["gpsimd"]
        s_ld = p.sem("s_ld")
        s_lx = p.sem("s_lx")
        s_lc = p.sem("s_lc")
        s_ls = p.sem("s_ls")
        s_wv = [p.sem("s_wv0"), p.sem("s_wv1")]
        s_xs = p.sem("s_xs")
        s_qf = p.sem("s_qf")
        s_tb = [p.sem("s_tb0"), p.sem("s_tb1")]
        s_xn = p.sem("s_xn")
        s_xt = p.sem("s_xt")
        s_up = [p.sem("s_up0"), p.sem("s_up1")]
        s_gt = [p.sem("s_gt0"), p.sem("s_gt1")]
        s_dn = [p.sem("s_dn0"), p.sem("s_dn1")]
        s_qp = [p.sem("s_qp0"), p.sem("s_qp1")]
        s_wo = [p.sem("s_wo0"), p.sem("s_wo1")]
        s_t = p.sem("s_t")
        s_st = p.sem("s_st")
        s_kb = p.sem("s_kb")
        s_qb = p.sem("s_qb")
        s_sb1 = p.sem("s_sb1")
        s_sc2 = p.sem("s_sc2")
        s_ab = p.sem("s_ab")
        s_ax = p.sem("s_ax")
        cc1 = p.sem("cc1")
        cc2 = p.sem("cc2")
        cc3 = p.sem("cc3")

        for dst, src in ((idb, id_b_t), (idf, id_f_t), (onb, ones_b_t),
                         (onf, ones_f_t), (onc, ones_c_t), (epsc, eps_t)):
            p.emit("scalar", lambda e, dst=dst, src=src: e.dma_start(
                out=dst[:], in_=src.ap()), inc=s_ld, inc_k=16, dma=True)
        p.emit("scalar", lambda e: e.dma_start(out=bqo_sb[:], in_=bqo_in), inc=s_ld, inc_k=16, dma=True)
        p.emit("scalar", lambda e: e.dma_start(out=bkv_sb[:], in_=bkv_in), inc=s_ld, inc_k=16, dma=True)
        c_ld = p.emit("scalar", lambda e: e.dma_start(out=slot_sb[:], in_=slot_in), inc=s_ld, inc_k=16, dma=True)
        z1 = p.emit("vector", lambda e: e.memset(a_acc[:], 0.0))
        z2 = p.emit("vector", lambda e: e.memset(sc_acc[:], 0.0))

        kv_mm_done = {}
        kv_evac_done = {}
        xn_spill = 0
        a_add_done = 0
        sqrt_done = 0
        inv_copy_done = 0
        ev_done = 0
        tr_last_prev = 0
        kv_mm_prev_last = 0
        app_done_prev = 0
        rope_end_prev = 0
        sc_next = [0]
        qn_next = [0]
        qf_user = [None]
        qf_cnt = [0]
        sub_done = {}
        scan_add_last = [0]
        colc_done_box = [None]
        rslot_box = []

        def emit_scale_unit():
            i = sc_next[0]
            sc_next[0] += 1
            m = i // 8
            sl = i % 2
            w = [qf_user[0]] if qf_user[0] else []
            qf_cnt[0] = p.emit("sync", lambda e, i=i, sl=sl: e.dma_start(
                out=qf[:, sl, :], in_=scs_in[i]), wait=w, inc=s_qf, inc_k=16, dma=True)
            ab = p.emit("scalar", lambda e, sl=sl: e.activation(
                m1b[:, 0, :], qf[:, sl, :], AF.Abs, accum_out=ab_c[:]),
                wait=[(s_qf, qf_cnt[0])])
            qf_user[0] = (s_s, ab)
            scan_add_last[0] = p.emit("vector", lambda e, m=m: e.tensor_tensor(
                sc_acc[:, m:m + 1], sc_acc[:, m:m + 1], ab_c[:], ALU.add),
                wait=[(s_s, ab), (s_v, z2)])

        def q_dstq(gu):
            if gu < 2 * ECH:
                return blob_ug[:, gu * 2048:(gu + 1) * 2048]
            v = gu - 2 * ECH
            return blob_dn[:, v * 2048:(v + 1) * 2048]

        def emit_quant_unit():
            u = qn_next[0]
            qn_next[0] += 1
            m = u // 32
            colc_done = colc_done_box[0]
            sl = u % 2
            w = [qf_user[0]] if qf_user[0] else []
            qf_cnt[0] = p.emit("sync", lambda e, u=u, sl=sl: e.dma_start(
                out=qf[:, sl, :], in_=qsrc_in[u]), wait=w, inc=s_qf, inc_k=16, dma=True)
            ws_t2 = []
            if u >= 2:
                ws_t2.append((s_tb[u % 2], 16 * (u // 2)))
            path = u % 3
            if path == 1:
                p.emit("scalar", lambda e, sl=sl, m=m: e.activation(
                    m1b[:, sl, :], qf[:, sl, :], AF.Sign, bias=colc[:, 5 + m:6 + m]),
                    wait=[(s_qf, qf_cnt[0]), (s_v, colc_done)])
                sg2 = p.emit("scalar", lambda e, sl=sl, m=m: e.activation(
                    m2b[:, sl, :], qf[:, sl, :], AF.Sign, bias=colc[:, m:m + 1]))
                qf_user[0] = (s_s, sg2)
                sub_done[u] = (s_g, p.emit("gpsimd", lambda e, sl=sl: e.tensor_tensor(
                    tb[:, sl, :], m1b[:, sl, :], m2b[:, sl, :], ALU.add),
                    wait=ws_t2 + [(s_s, sg2)]))
            else:
                p.emit("vector", lambda e, sl=sl, m=m: e.tensor_scalar(
                    m1b[:, sl, :], qf[:, sl, :], colc[:, m:m + 1], 2.0, ALU.is_gt, ALU.mult),
                    wait=[(s_qf, qf_cnt[0]), (s_v, colc_done)])
                lastq = p.emit("vector", lambda e, sl=sl, m=m: e.tensor_scalar(
                    m2b[:, sl, :], qf[:, sl, :], colc[:, 5 + m:6 + m], 2.0, ALU.is_lt, ALU.mult))
                qf_user[0] = (s_v, lastq)
                if path == 0:
                    sub_done[u] = (s_v, p.emit("vector", lambda e, sl=sl: e.tensor_tensor(
                        tb[:, sl, :], m1b[:, sl, :], m2b[:, sl, :], ALU.subtract),
                        wait=ws_t2))
                else:
                    sub_done[u] = (s_g, p.emit("gpsimd", lambda e, sl=sl: e.tensor_tensor(
                        tb[:, sl, :], m1b[:, sl, :], m2b[:, sl, :], ALU.subtract),
                        wait=ws_t2 + [(s_v, lastq)]))
            lo = m * 64 + (u % 32)
            hi = m * 64 + 32 + (u % 32)

            def fn_store(e, sl=sl, lo=lo, hi=hi):
                if not rslot_box:
                    rslot_box.append(e.register("rqs").__enter__())
                    e.reg_load(rslot_box[0], slot_sb[0:1, 0:1])
                r = rslot_box[0]
                with e.If_eq(r, 0):
                    e.dma_start(out=q_dstq(lo), in_=tb[:, sl, :]).then_inc(
                        s_tb[sl].h, 16)
                with e.Else():
                    e.dma_start(out=q_dstq(hi), in_=tb[:, sl, :]).then_inc(
                        s_tb[sl].h, 16)
                return e.nop()
            p.emit("scalar", fn_store,
                   wait=[sub_done[u], (s_ld, c_ld)], dma=True)
            s_tb[sl].bump(16)

        xld_t = {}
        cld_t = {}
        sld_t = {}
        app_t = {-1: 0}
        tr_box = [0]

        def emit_ln(t):
            xld = xld_t[t] = p.emit("scalar", lambda e, t=t: e.dma_start(
                out=xt1[:], in_=x_in[t * T:(t + 1) * T, :].rearrange("(g p) d -> p g d", p=128)),
                wait=[(s_v, app_t[t - 1])], inc=s_lx, inc_k=16, dma=True)
            sum_done = ss_done = 0
            for g in range(4):
                sum_done = p.emit("vector", lambda e, g=g: e.reduce_sum(
                    stat[:, g:g + 1], xt1[:, g, :], AX.X), wait=[(s_lx, xld)])
                ss_done = p.emit("scalar", lambda e, g=g: e.activation(
                    sqs[:], xt1[:, g, :], AF.Square, accum_out=stat[:, 4 + g:5 + g]),
                    wait=[(s_lx, xld)])
            p.emit("vector", lambda e: e.tensor_scalar(
                stat[:, 0:4], stat[:, 0:4], 1.0 / D, None, ALU.mult),
                wait=[(s_v, sum_done)])
            p.emit("vector", lambda e: e.tensor_scalar(
                stat[:, 4:8], stat[:, 4:8], 1.0 / D, None, ALU.mult),
                wait=[(s_s, ss_done)])
            p.emit("vector", lambda e: e.tensor_tensor(
                stat[:, 8:12], stat[:, 0:4], stat[:, 0:4], ALU.mult))
            vstat = p.emit("vector", lambda e: e.tensor_tensor(
                stat[:, 8:12], stat[:, 4:8], stat[:, 8:12], ALU.subtract))
            sstat = p.emit("scalar", lambda e: e.activation(
                stat[:, 12:16], stat[:, 8:12], AF.Sqrt, bias=epsc[:]),
                wait=[(s_v, vstat), (s_ld, c_ld)])
            vstat = p.emit("vector", lambda e: e.reciprocal(
                stat[:, 12:16], stat[:, 12:16]),
                wait=[(s_s, sstat)])
            last = 0
            for g in range(4):
                last = p.emit("vector", lambda e, g=g: e.tensor_scalar(
                    xnb[:, g, :], xt1[:, g, :], stat[:, g:g + 1],
                    stat[:, 12 + g:13 + g], ALU.subtract, ALU.mult),
                    wait=[(s_v, vstat), (s_t, tr_box[0])])
            app_t[t] = last

        emit_ln(0)
        cld_t[0] = p.emit("scalar", lambda e: e.dma_start(
            out=cosb[:], in_=cos_in[:, 0:T].rearrange("(g p) s -> p g s", p=128)),
            inc=s_lc, inc_k=16, dma=True)
        sld_t[0] = p.emit("scalar", lambda e: e.dma_start(
            out=sinb[:], in_=sin_in[:, 0:T].rearrange("(g p) s -> p g s", p=128)),
            inc=s_ls, inc_k=16, dma=True)

        for tt in range(NT):
            cld2 = cld_t[tt]
            sld2 = sld_t[tt]
            for kc in range(DK):
                tr_w = [(s_v, app_t[tt]), (s_xs, xn_spill), (s_ld, c_ld),
                        (s_t, kv_mm_prev_last)]
                if kc >= 1 or tt > 0:
                    tr_w.append((s_v, ev_done))
                tr_last = 0
                for g in range(4):
                    tr_last = p.emit("tensor", lambda e, kc=kc, g=g: e.transpose(
                        B7[:, g * 128:(g + 1) * 128],
                        xnb[:, g, kc * 128:(kc + 1) * 128], idb[:]),
                        wait=tr_w if g == 0 else (), inc=s_t)
                ev_done = p.emit("vector", lambda e, kc=kc: e.tensor_copy(
                    xnt[:, kc, :], B7[:]), wait=[(s_t, tr_last)])
            xn_done = ev_done
            tr_box[0] = tr_last
            xn_spill = p.emit("scalar", lambda e, tt=tt: e.dma_start(
                out=xn_dram[:, :, tt * T:(tt + 1) * T], in_=xnt[:]),
                wait=[(s_v, xn_done)], inc=s_xs, inc_k=16, dma=True)

            for oc in range(32):
                step = tt * 32 + oc
                b = oc % 2
                if tt == 1 and oc == 8:
                    p.emit("gpsimd", lambda e: e.collective_compute(
                        "AllReduce", ALU.add,
                        replica_groups=[list(range(NC))],
                        ins=[scb_in.ap().opt()], outs=[scb_out.ap().opt()]),
                        wait=[(s_sb1, 16)], inc=cc1, inc_k=1, dma=True)
                wld = []
                if step >= 2:
                    wld.append((s_t, kv_mm_done[step - 2]))
                if tt == 0:
                    wl = p.emit("gpsimd", lambda e, b=b, oc=oc: e.dma_start(
                        out=kwb[:, b, :], in_=kvp_in[oc]),
                        wait=wld + ([(s_kb, 16 * (oc - 1))] if oc >= 2 else []),
                        inc=s_wv[b], inc_k=16, dma=True)
                    p.emit("scalar", lambda e, b=b, oc=oc: e.dma_start(
                        out=kvb_bf[oc], in_=kwb[:, b, :]),
                        wait=[(s_wv[b], wl)], inc=s_kb, inc_k=16, dma=True)
                else:
                    wld.append((s_kb, 16 * (oc + 1)))
                    wl = p.emit("sync" if oc % 2 == 0 else "scalar",
                                lambda e, b=b, oc=oc: e.dma_start(
                        out=kwb[:, b, :], in_=kvb_bf[oc]),
                        wait=wld, inc=s_wv[b], inc_k=16, dma=True)
                pb = B[b]
                mmw = [(s_wv[b], wl), (s_v, xn_done)]
                if step >= 2:
                    mmw.append((s_v, kv_evac_done[step - 2]))
                last_mm = 0
                for kc in range(DK):
                    last_mm = p.emit("tensor", lambda e, b=b, kc=kc, pb=pb: e.matmul(
                        pb[:], kwb[:, b, kc * 128:(kc + 1) * 128], xnt[:, kc, :],
                        start=(kc == 0), stop=(kc == DK - 1)),
                        wait=mmw if kc == 0 else (), inc=s_t)
                kv_mm_done[step] = last_mm
                dst, cidx = (ksb, oc) if oc < 16 else (vsb, oc - 16)
                kv_evac_done[step] = p.emit(
                    "vector", lambda e, pb=pb, dst=dst, cidx=cidx, oc=oc: e.tensor_scalar(
                        dst[:, cidx, :], pb[:], bkv_sb[:, oc:oc + 1], None, ALU.add),
                    wait=[(s_t, last_mm), (s_ld, c_ld)])
                if tt == 0 and oc < 24:
                    emit_scale_unit()
                if tt >= 2 and oc % 2 == 1 and qn_next[0] < 48 * (tt - 1):
                    for _ in range(3):
                        if qn_next[0] < 96:
                            emit_quant_unit()
            kv_mm_prev_last = kv_mm_done[tt * 32 + 31]
            k_evac = {oc: kv_evac_done[tt * 32 + oc] for oc in range(16)}
            v_evac = {oc: kv_evac_done[tt * 32 + 16 + oc] for oc in range(16)}

            if tt == 0:
                p.emit("scalar", lambda e: e.dma_start(
                    out=scb_in.ap(), in_=sc_acc[:]),
                    wait=[(s_v, scan_add_last[0])], inc=s_sb1, inc_k=16, dma=True)

            if tt + 1 < NT:
                emit_ln(tt + 1)

            rope_done = {}
            for c in range(4):
                w = [(s_v, max(k_evac[c], k_evac[c + 4])), (s_lc, cld2), (s_ls, sld2)]
                p.emit("vector", lambda e, c=c: e.tensor_tensor(
                    rsc1[:], ksb[:, c, :], cosb[:, c, :], ALU.mult), wait=w)
                p.emit("vector", lambda e, c=c: e.tensor_tensor(
                    rsc2[:], ksb[:, c + 4, :], sinb[:, c, :], ALU.mult))
                rope_done[c] = p.emit("vector", lambda e, c=c: e.tensor_tensor(
                    krot[:, c, :], rsc1[:], rsc2[:], ALU.subtract))
                p.emit("vector", lambda e, c=c: e.tensor_tensor(
                    rsc1[:], ksb[:, c + 4, :], cosb[:, c, :], ALU.mult))
                p.emit("vector", lambda e, c=c: e.tensor_tensor(
                    rsc2[:], ksb[:, c, :], sinb[:, c, :], ALU.mult))
                rope_done[c + 4] = p.emit("vector", lambda e, c=c: e.tensor_tensor(
                    krot[:, c + 4, :], rsc1[:], rsc2[:], ALU.add))
            rope_end_prev = rope_done[7]
            if tt + 1 < NT:
                cld_t[tt + 1] = p.emit("scalar", lambda e, tt=tt: e.dma_start(
                    out=cosb[:], in_=cos_in[:, (tt + 1) * T:(tt + 2) * T].rearrange("(g p) s -> p g s", p=128)),
                    wait=[(s_v, rope_end_prev)], inc=s_lc, inc_k=16, dma=True)
                sld_t[tt + 1] = p.emit("scalar", lambda e, tt=tt: e.dma_start(
                    out=sinb[:], in_=sin_in[:, (tt + 1) * T:(tt + 2) * T].rearrange("(g p) s -> p g s", p=128)),
                    inc=s_ls, inc_k=16, dma=True)

            def ksrc(c):
                return krot[:, c, :] if c < 8 else ksb[:, c, :]

            sq_done = {}
            mm_ss = {}
            last_ss = 0
            for c in range(DK):
                w = [(s_v, rope_done[c])] if c < 8 else [(s_v, k_evac[c])]
                if c >= 2:
                    w.append((s_t, mm_ss[c - 2]))
                sq_done[c] = p.emit("scalar", lambda e, c=c: e.activation(
                    ksq[:, c % 2, :], ksrc(c), AF.Square), wait=w)
                w2 = [(s_s, sq_done[c])]
                if tt > 0 and c == 0:
                    w2.append((s_s, sqrt_done))
                last_ss = mm_ss[c] = p.emit("tensor", lambda e, c=c: e.matmul(
                    B[5][0:1, :], onb[:], ksq[:, c % 2, :],
                    start=(c == 0), stop=(c == DK - 1)),
                    wait=w2, inc=s_t)
            sqrt_done = p.emit("scalar", lambda e: e.activation(
                nrm[0:1, 0:T], B[5][0:1, :], AF.Sqrt),
                wait=[(s_t, last_ss), (s_v, a_add_done)])
            vmax = p.emit("vector", lambda e: e.tensor_scalar(
                nrm[0:1, 0:T], nrm[0:1, 0:T], 1e-12, None, ALU.max),
                wait=[(s_s, sqrt_done)])
            vinv = p.emit("vector", lambda e: e.reciprocal(
                nrm[0:1, 0:T], nrm[0:1, 0:T]))
            bc_w = [(s_v, vinv)]
            if tt > 0:
                bc_w.append((s_v, inv_copy_done))
            bc_done = p.emit("tensor", lambda e: e.matmul(
                B[4][:], onf[:], nrm[0:1, 0:T], start=True, stop=True),
                wait=bc_w, inc=s_t)
            inv_copy_done = p.emit("vector", lambda e: e.tensor_copy(
                invb[:], B[4][:]), wait=[(s_t, bc_done)])

            for c in range(DK):
                w = [(s_v, max(inv_copy_done,
                               v_evac[c], k_evac[c] if c >= 8 else 0))]
                p.emit("vector", lambda e, c=c: e.tensor_tensor(
                    vs_scr[:], vsb[:, c, :], invb[:], ALU.mult), wait=w)
                p.emit("vector", lambda e, c=c: e.scalar_tensor_tensor(
                    pr_scr[:], ksrc(c), 1.0, vs_scr[:], ALU.bypass, ALU.mult,
                    accum_out=a_tmp[:, c:c + 1]))
            a_add_done = p.emit("vector", lambda e: e.tensor_tensor(
                a_acc[:], a_acc[:], a_tmp[:], ALU.add), wait=[(s_v, z1)])

            if tt == 1:
                p.emit("scalar", lambda e: e.dma_start(
                    out=spp[:], in_=scb_out.ap()),
                    wait=[(cc1, 1)], inc=s_sc2, inc_k=16, dma=True)
                red_mm = p.emit("tensor", lambda e: e.matmul(
                    B[6][0:1, 0:3], onc[:], spp[:], start=True, stop=True),
                    wait=[(s_sc2, 16), (s_ld, c_ld)], inc=s_t)
                p.emit("vector", lambda e: e.tensor_copy(sctot[:, 0:3], B[6][0:1, 0:3]),
                       wait=[(s_t, red_mm)])
                p.emit("vector", lambda e: e.tensor_scalar(
                    sctot[:, 0:3], sctot[:, 0:3], 1.0 / 16777216.0, 1e-8, ALU.mult, ALU.add))
                p.emit("vector", lambda e: e.tensor_scalar(
                    pk[:, 0:3], sctot[:, 0:3], 0.5, None, ALU.mult))
                p.emit("vector", lambda e: e.tensor_scalar(
                    pk[:, 3:4], sctot[:, 0:1], 0.5, None, ALU.mult))
                p.emit("vector", lambda e: e.tensor_tensor(
                    pk[:, 4:5], sctot[:, 1:2], sctot[:, 2:3], ALU.mult))
                pk_done = p.emit("vector", lambda e: e.tensor_scalar(
                    pk[:, 4:5], pk[:, 4:5], 0.25, None, ALU.mult))
                bc6_mm = p.emit("tensor", lambda e: e.matmul(
                    B[6][0:128, 0:5], onf[:], pk[0:1, 0:5], start=True, stop=True),
                    wait=[(s_v, pk_done)], inc=s_t)
                p.emit("vector", lambda e: e.tensor_copy(colc[:, 0:5], B[6][0:128, 0:5]),
                       wait=[(s_t, bc6_mm)])
                colc_done_box[0] = p.emit("vector", lambda e: e.tensor_scalar(
                    colc[:, 5:8], colc[:, 0:3], -1.0, None, ALU.mult))

        while qn_next[0] < 96:
            emit_quant_unit()
        quant_end_s = s_s.n
        quant_end_v = s_v.n
        quant_end_g = s_g.n
        tb0_end = s_tb[0].n
        tb1_end = s_tb[1].n
        ph1_s_end = s_s.n
        ph1_t_end = s_t.n
        kv_evac_last = kv_evac_done[NT * 32 - 1]
        ph1_sqrt_last = sqrt_done
        ph1_inv_last = inv_copy_done

        p.emit("gpsimd", lambda e: e.collective_compute(
            "AllReduce", ALU.add,
            replica_groups=[[0, 1], [2, 3], [4, 5], [6, 7]],
            ins=[bb_in.ap().opt()], outs=[bb_out.ap().opt()]),
            wait=[(s_tb[0], tb0_end), (s_tb[1], tb1_end)],
            inc=cc3, inc_k=1, dma=True)

        p.emit("scalar", lambda e: e.dma_start(
            out=ab_in.ap(), in_=a_acc[:]),
            wait=[(s_v, a_add_done)], inc=s_ab, inc_k=16, dma=True)
        p.emit("gpsimd", lambda e: e.collective_compute(
            "AllReduce", ALU.add,
            replica_groups=[[0, 1, 2, 3], [4, 5, 6, 7]],
            ins=[ab_in.ap().opt()], outs=[ab_out.ap().opt()]),
            wait=[(s_ab, 16)], inc=cc2, inc_k=1, dma=True)
        p.emit("gpsimd", lambda e: e.dma_start(
            out=a_sb[:], in_=ab_out.ap()),
            wait=[(cc2, 1)], inc=s_ax, inc_k=16, dma=True)

        xsl = [xnt, xnt2]
        q_mm_done = {-1: 0, -2: 0}
        wout_mm_done = {-1: 0}
        down_all_done = {-1: 0}
        gate_all_done = {-1: 0}
        rope_end = {-1: rope_end_prev}
        prod_end = {-1: 0, -2: 0}
        outadd_end = {-1: 0}
        outstore = {-1: 0}
        qev_last_prev = 0
        stt_last_prev = 0
        gev_last_prev = 0
        up_mm_prev = gt_mm_prev = dn_mm_prev = 0
        colc_done = colc_done_box[0]
        xnt_ld = {}

        def emit_xnt_ld(t):
            sl = t % 2
            wld = [(s_t, q_mm_done[t - 2]), (s_v, prod_end[t - 2]),
                   (s_xs, 16 * NT)]
            if t == 0:
                wld.append((s_t, ph1_t_end))
            if t == 1:
                wld += [(s_s, quant_end_s), (s_v, quant_end_v),
                        (s_g, quant_end_g),
                        (s_tb[0], tb0_end), (s_tb[1], tb1_end)]
            xnt_ld[t] = p.emit("scalar", lambda e, t=t, sl=sl: e.dma_start(
                out=xsl[sl][:], in_=xn_dram[:, :, t * T:(t + 1) * T]),
                wait=wld, inc=s_xn, inc_k=16, dma=True)

        emit_xnt_ld(0)
        for t in range(NT):
            xv = xsl[t % 2]
            cld2 = p.emit("scalar", lambda e, t=t: e.dma_start(
                out=cosb[:], in_=cos_in[:, t * T:(t + 1) * T].rearrange("(g p) s -> p g s", p=128)),
                wait=[(s_v, rope_end[t - 1])], inc=s_lc, inc_k=16, dma=True)
            sld2 = p.emit("scalar", lambda e, t=t: e.dma_start(
                out=sinb[:], in_=sin_in[:, t * T:(t + 1) * T].rearrange("(g p) s -> p g s", p=128)),
                inc=s_ls, inc_k=16, dma=True)

            up_ld = {}
            gt_ld = {}
            up_mm = {}
            gt_mm = {}
            silu_done = {}
            hid_done = {}
            for oc in range(ECH):
                if oc % 2 == 0:
                    j = oc // 2
                    wld = [(s_v, outadd_end[t - 1])]
                    if t == 0:
                        wld.append((cc3, 1))
                        wld.append((s_v, a_add_done))
                    if j >= 2:
                        wld.append((s_t, up_mm[oc - 3]))
                    elif t > 0:
                        wld.append((s_t, up_mm_prev))
                    up_ld[j] = p.emit("sync", lambda e, j=j: e.dma_start(
                        out=upsl[:, j % 2, :, :],
                        in_=blob_ug[:, (2 * j) * 2048:(2 * j + 2) * 2048]),
                        wait=wld, inc=s_up[j % 2], inc_k=16, dma=True)
                    wld2 = [(s_v, outadd_end[t - 1])]
                    if t == 0:
                        wld2.append((cc3, 1))
                        wld2.append((s_v, a_add_done))
                        wld2.append((s_s, ph1_s_end))
                    if j >= 2:
                        wld2.append((s_t, gt_mm[oc - 3]))
                    elif t > 0:
                        wld2.append((s_t, gt_mm_prev))
                    gt_ld[j] = p.emit("sync", lambda e, j=j: e.dma_start(
                        out=gtsl[:, j % 2, :, :],
                        in_=blob_ug[:, (ECH + 2 * j) * 2048:(ECH + 2 * j + 2) * 2048]),
                        wait=wld2, inc=s_gt[j % 2], inc_k=16, dma=True)
                j = oc // 2
                bu = B[oc % 2]
                wmm = [(s_up[j % 2], up_ld[j]), (s_xn, xnt_ld[t])]
                if oc >= 2:
                    wmm.append((s_s, silu_done[oc - 2]))
                elif t > 0:
                    wmm.append((s_s, qev_last_prev))
                else:
                    wmm.append((s_v, kv_evac_last))
                last = 0
                for kc in range(DK):
                    last = p.emit("tensor", lambda e, oc=oc, kc=kc, bu=bu, j=j, xv=xv: e.matmul(
                        bu[:], upsl[:, j % 2, oc % 2, kc * 128:(kc + 1) * 128],
                        xv[:, kc, :], start=(kc == 0), stop=(kc == DK - 1)),
                        wait=wmm if kc == 0 else (), inc=s_t)
                up_mm[oc] = last
                ws = [(s_t, up_mm[oc]), (s_v, colc_done)]
                if oc >= 2:
                    ws.append((s_v, hid_done[oc - 2]))
                elif t == 0:
                    ws.append((s_v, quant_end_v))
                    ws.append((s_g, quant_end_g))
                silu_done[oc] = p.emit("scalar", lambda e, oc=oc, bu=bu: e.activation(
                    silu_u[:, oc % 2, :], bu[:], AF.Silu, scale=colc[:, 3:4]), wait=ws)
                bg = B[2 + oc % 2]
                wmm2 = [(s_gt[j % 2], gt_ld[j])]
                if oc >= 2:
                    wmm2.append((s_v, hid_done[oc - 2]))
                elif t > 0:
                    wmm2.append((s_v, stt_last_prev))
                last = 0
                for kc in range(DK):
                    last = p.emit("tensor", lambda e, oc=oc, kc=kc, bg=bg, j=j, xv=xv: e.matmul(
                        bg[:], gtsl[:, j % 2, oc % 2, kc * 128:(kc + 1) * 128],
                        xv[:, kc, :], start=(kc == 0), stop=(kc == DK - 1)),
                        wait=wmm2 if kc == 0 else (), inc=s_t)
                gt_mm[oc] = last
                wh = [(s_t, gt_mm[oc]), (s_s, silu_done[oc])]
                if oc == 0:
                    wh.append((s_t, wout_mm_done[t - 1]))
                    if t == 0:
                        wh.append((s_s, ph1_s_end))
                hid_done[oc] = p.emit("vector", lambda e, oc=oc, bg=bg: e.tensor_tensor(
                    hid[:, oc, :], bg[:], silu_u[:, oc % 2, :], ALU.mult), wait=wh)
            gate_all_done[t] = gt_mm[ECH - 1]
            hid_all = hid_done[ECH - 1]

            dn_ld = {}
            dn_mm = {}
            g_evac = {}
            for dc in range(DK):
                wld = [(s_st, outstore[t - 1])]
                if t == 0:
                    wld.append((s_t, ph1_t_end))
                    wld.append((s_v, quant_end_v))
                    wld.append((s_s, quant_end_s))
                    wld.append((cc3, 1))
                    wld.append((s_v, a_add_done))
                if dc >= 2:
                    wld.append((s_t, dn_mm[dc - 2]))
                elif t > 0:
                    wld.append((s_t, dn_mm_prev))
                dn_ld[dc] = p.emit("scalar", lambda e, dc=dc: e.dma_start(
                    out=dnsl[:, dc % 2, :], in_=blob_dn[:, dc * 8192:(dc + 1) * 8192]),
                    wait=wld, inc=s_dn[dc % 2], inc_k=16, dma=True)
                bd = B[4 + dc % 2]
                wmm = [(s_dn[dc % 2], dn_ld[dc]), (s_v, hid_all)]
                if dc >= 2:
                    wmm.append((s_s, g_evac[dc - 2]))
                elif t > 0:
                    wmm.append((s_s, gev_last_prev))
                else:
                    wmm.append((s_s, ph1_sqrt_last))
                    wmm.append((s_v, ph1_inv_last))
                last = 0
                for ec in range(ECH):
                    last = p.emit("tensor", lambda e, dc=dc, ec=ec, bd=bd: e.matmul(
                        bd[:], dnsl[:, dc % 2, ec * 128:(ec + 1) * 128],
                        hid[:, ec, :], start=(ec == 0), stop=(ec == ECH - 1)),
                        wait=wmm if ec == 0 else (), inc=s_t)
                dn_mm[dc] = last
                wg = [(s_t, dn_mm[dc])]
                if dc == 0:
                    wg.append((s_v, prod_end[t - 1]))
                g_evac[dc] = p.emit("scalar", lambda e, dc=dc, bd=bd: e.activation(
                    g_sb[:, dc, :], bd[:], AF.Silu, scale=colc[:, 4:5]), wait=wg)
            down_all_done[t] = dn_mm[DK - 1]

            if t + 1 < NT:
                emit_xnt_ld(t + 1)

            qp_ld = {}
            qmm = {}
            q_evac = {}
            for oc in range(DK):
                wld = [(s_s, quant_end_s), (s_v, quant_end_v), (s_g, quant_end_g)]
                if oc >= 2:
                    wld.append((s_t, qmm[oc - 2]))
                else:
                    wld.append((s_t, q_mm_done[t - 1]))
                if t == 0:
                    if oc >= 2:
                        wld.append((s_qb, 16 * (oc - 1)))
                    qp_ld[oc] = p.emit("gpsimd", lambda e, oc=oc: e.dma_start(
                        out=qpb[:, oc % 2, :], in_=qp_in[oc]),
                        wait=wld, inc=s_qp[oc % 2], inc_k=16, dma=True)
                    p.emit("sync", lambda e, oc=oc: e.dma_start(
                        out=qwb_bf[oc], in_=qpb[:, oc % 2, :]),
                        wait=[(s_qp[oc % 2], qp_ld[oc])], inc=s_qb, inc_k=16, dma=True)
                else:
                    wld.append((s_qb, 16 * (oc + 1)))
                    qp_ld[oc] = p.emit("scalar", lambda e, oc=oc: e.dma_start(
                        out=qpb[:, oc % 2, :], in_=qwb_bf[oc]),
                        wait=wld, inc=s_qp[oc % 2], inc_k=16, dma=True)
                bq_ = B[oc % 2]
                wmm = [(s_qp[oc % 2], qp_ld[oc]), (s_s, silu_done[ECH - 1])]
                if oc >= 2:
                    wmm.append((s_s, q_evac[oc - 2]))
                last = 0
                for kc in range(DK):
                    last = p.emit("tensor", lambda e, oc=oc, kc=kc, bq_=bq_, xv=xv: e.matmul(
                        bq_[:], qpb[:, oc % 2, kc * 128:(kc + 1) * 128],
                        xv[:, kc, :], start=(kc == 0), stop=(kc == DK - 1)),
                        wait=wmm if kc == 0 else (), inc=s_t)
                qmm[oc] = last
                wq = [(s_t, max(qmm[oc], down_all_done[t]))]
                q_evac[oc] = p.emit("scalar", lambda e, oc=oc, bq_=bq_: e.activation(
                    q_sb[:, oc, :], bq_[:], AF.Identity, bias=bqo_sb[:, oc:oc + 1]), wait=wq)
            q_mm_done[t] = qmm[DK - 1]

            rope_done = {}
            for c in range(4):
                w = [(s_s, max(q_evac[c], q_evac[c + 4])), (s_lc, cld2), (s_ls, sld2),
                     (s_t, down_all_done[t])]
                if t == 0 and c == 0:
                    w.append((s_g, quant_end_g))
                p.emit("vector", lambda e, c=c: e.tensor_tensor(
                    s_scr[:], q_sb[:, c, :], cosb[:, c, :], ALU.mult), wait=w)
                p.emit("vector", lambda e, c=c: e.tensor_tensor(
                    pr2[:], q_sb[:, c + 4, :], sinb[:, c, :], ALU.mult))
                rope_done[c] = p.emit("vector", lambda e, c=c: e.tensor_tensor(
                    qrot[:, c, :], s_scr[:], pr2[:], ALU.subtract))
                p.emit("vector", lambda e, c=c: e.tensor_tensor(
                    s_scr[:], q_sb[:, c + 4, :], cosb[:, c, :], ALU.mult))
                p.emit("vector", lambda e, c=c: e.tensor_tensor(
                    pr2[:], q_sb[:, c, :], sinb[:, c, :], ALU.mult))
                rope_done[c + 4] = p.emit("vector", lambda e, c=c: e.tensor_tensor(
                    qrot[:, c + 4, :], s_scr[:], pr2[:], ALU.add))
            rope_end[t] = rope_done[7]

            def qsrc(c):
                return qrot[:, c, :] if c < 8 else q_sb[:, c, :]

            y_done = 0
            for c in range(DK):
                w = [(s_s, q_evac[c])] if c >= 8 else []
                if c == 0:
                    w.append((s_ax, 16))
                y_done = p.emit("vector", lambda e, c=c: e.tensor_scalar(
                    y_sb[:, c, :], qsrc(c), a_sb[:, c:c + 1], None, ALU.mult), wait=w)

            xt_ld = p.emit("scalar", lambda e, t=t: e.dma_start(
                out=xt4[:], in_=x_in[t * T:(t + 1) * T, :].rearrange("(g p) d -> p g d", p=128)),
                wait=[(s_t, gate_all_done[t])], inc=s_xt, inc_k=16, dma=True)

            wo_ld = {}
            womm = {}
            stt_done = {}
            oadd = {}
            for oc in range(DK):
                wld = [(s_v, quant_end_v), (s_s, quant_end_s), (s_g, quant_end_g)]
                if oc >= 2:
                    wld.append((s_t, womm[oc - 2]))
                else:
                    wld.append((s_t, wout_mm_done[t - 1]))
                if t == 0:
                    if oc >= 2:
                        wld.append((s_qb, 16 * (16 + oc - 1)))
                    wo_ld[oc] = p.emit("gpsimd", lambda e, oc=oc: e.dma_start(
                        out=wob[:, oc % 2, :], in_=wo_in[oc]),
                        wait=wld, inc=s_wo[oc % 2], inc_k=16, dma=True)
                    p.emit("sync", lambda e, oc=oc: e.dma_start(
                        out=qwb_bf[16 + oc], in_=wob[:, oc % 2, :]),
                        wait=[(s_wo[oc % 2], wo_ld[oc])], inc=s_qb, inc_k=16, dma=True)
                else:
                    wld.append((s_qb, 16 * (16 + oc + 1)))
                    wo_ld[oc] = p.emit("scalar", lambda e, oc=oc: e.dma_start(
                        out=wob[:, oc % 2, :], in_=qwb_bf[16 + oc]),
                        wait=wld, inc=s_wo[oc % 2], inc_k=16, dma=True)
                bo_ = B[2 + oc % 2]
                wmm = [(s_wo[oc % 2], wo_ld[oc]), (s_v, y_done), (s_v, hid_done[ECH - 1])]
                if oc >= 2:
                    wmm.append((s_v, stt_done[oc - 2]))
                last = 0
                for kc in range(DK):
                    last = p.emit("tensor", lambda e, oc=oc, kc=kc, bo_=bo_: e.matmul(
                        bo_[:], wob[:, oc % 2, kc * 128:(kc + 1) * 128],
                        y_sb[:, kc, :], start=(kc == 0), stop=(kc == DK - 1)),
                        wait=wmm if kc == 0 else (), inc=s_t)
                womm[oc] = last
                stt_done[oc] = p.emit("vector", lambda e, oc=oc, bo_=bo_, xv=xv: e.scalar_tensor_tensor(
                    s_scr[:], bo_[:], bqo_sb[:, 16 + oc:17 + oc], xv[:, oc, :],
                    ALU.add, ALU.add), wait=[(s_t, womm[oc])])
                prod_d = p.emit("vector", lambda e, oc=oc: e.tensor_tensor(
                    pr2[:], s_scr[:], g_sb[:, oc, :], ALU.mult))
                tr_w = [(s_v, prod_d), (s_ld, c_ld)]
                if oc >= 1:
                    tr_w.append((s_v, oadd[oc - 1]))
                else:
                    tr_w.append((s_v, outadd_end[t - 1]))
                    if t == 0:
                        tr_w.append((s_v, colc_done))
                tr_last = 0
                for g in range(4):
                    tr_last = p.emit("tensor", lambda e, g=g: e.transpose(
                        B[6][:, g * 128:(g + 1) * 128], pr2[:, g * 128:(g + 1) * 128], idf[:]),
                        wait=tr_w if g == 0 else (), inc=s_t)
                wadd = [(s_t, tr_last), (s_xt, xt_ld)]
                if oc == 0:
                    wadd.append((s_t, down_all_done[t]))
                    wadd.append((s_st, outstore[t - 1]))
                oadd[oc] = p.emit("vector", lambda e, oc=oc: e.tensor_tensor(
                    out_sb[:, :, oc * 128:(oc + 1) * 128],
                    B[6][:].rearrange("p (g c) -> p g c", g=4),
                    xt4[:, :, oc * 128:(oc + 1) * 128], ALU.add), wait=wadd)
            wout_mm_done[t] = womm[DK - 1]
            prod_end[t] = stt_done[DK - 1] + 1
            outadd_end[t] = oadd[DK - 1]

            outstore[t] = p.emit("sync", lambda e, t=t: e.dma_start(
                out=out_d[t * T:(t + 1) * T, :].rearrange("(g p) d -> p g d", p=128),
                in_=out_sb[:]), wait=[(s_v, outadd_end[t])], inc=s_st, inc_k=16, dma=True)
            qev_last_prev = q_evac[DK - 1]
            stt_last_prev = stt_done[DK - 1]
            gev_last_prev = g_evac[DK - 1]
            up_mm_prev = up_mm[ECH - 1]
            gt_mm_prev = gt_mm[ECH - 1]
            dn_mm_prev = dn_mm[DK - 1]

        p.barrier_wait("scalar", (s_st, s_st.n))
        p.replay()
    return nc


def _panels(WT, nkc, noc):
    return np.ascontiguousarray(
        WT.reshape(nkc, 128, noc, 128).transpose(2, 1, 0, 3).reshape(noc, 128, nkc * 128))


def _perm():
    return np.concatenate([np.arange(0, 1024, 2), np.arange(1, 1024, 2),
                           np.arange(1024, 2048)])


def _bias_cols(b):
    n = b.shape[0] // 128
    return np.ascontiguousarray(b.reshape(n, 128).T)


def host_prep(L0_attn_X, L0_attn_W_in, L0_attn_b_in, L0_attn_W_out,
              L0_attn_b_out, L0_ffn_W_up, L0_ffn_W_gate, L0_ffn_W_down):
    pm = _perm()
    f32 = np.float32
    Wq = np.ascontiguousarray(L0_attn_W_in[0:D][pm]).astype(f32, copy=False)
    Wk = np.ascontiguousarray(L0_attn_W_in[D:2 * D][pm]).astype(f32, copy=False)
    Wv = np.ascontiguousarray(L0_attn_W_in[2 * D:3 * D][pm]).astype(f32, copy=False)
    q_panels = _panels(np.ascontiguousarray(Wq.T), DK, DK)
    k_panels = _panels(np.ascontiguousarray(Wk.T), DK, DK)
    v_panels = _panels(np.ascontiguousarray(Wv.T), DK, DK)
    kv_panels = np.ascontiguousarray(np.concatenate([k_panels, v_panels], axis=0))
    wo_panels = _panels(np.ascontiguousarray(L0_attn_W_out[:, pm].T), DK, DK)
    up_panels = _panels(np.ascontiguousarray(L0_ffn_W_up.T), DK, ECH)
    gate_panels = _panels(np.ascontiguousarray(L0_ffn_W_gate.T), DK, ECH)
    down_panels = _panels(np.ascontiguousarray(L0_ffn_W_down.T), ECH, DK)

    bq = _bias_cols(np.ascontiguousarray(L0_attn_b_in[0:D][pm]))
    bk = _bias_cols(np.ascontiguousarray(L0_attn_b_in[D:2 * D][pm]))
    bv = _bias_cols(np.ascontiguousarray(L0_attn_b_in[2 * D:3 * D][pm]))
    bkv = np.ascontiguousarray(np.concatenate([bk, bv], axis=1))
    bo = _bias_cols(np.ascontiguousarray(L0_attn_b_out))

    X2 = np.ascontiguousarray(L0_attn_X.reshape(2 * 8192, D))

    tabs = []
    inv_freq32 = (1.0 / (10000.0 ** (np.arange(0, 1024, 2, dtype=np.float32) / 1024.0))).astype(np.float32)
    for c in range(NC):
        pos0 = (c % 4) * SC
        pos = np.arange(pos0, pos0 + SC, dtype=np.float32)
        ang = pos[None, :] * inv_freq32[:, None]
        tabs.append((np.cos(ang).astype(ml_dtypes.bfloat16),
                     np.sin(ang).astype(ml_dtypes.bfloat16)))

    dn_u = np.ascontiguousarray(
        down_panels.reshape(DK, 128, 4, 2048).transpose(0, 2, 1, 3).reshape(ECH, 128, 2048))

    bqo = np.ascontiguousarray(np.concatenate([bq, bo], axis=1))

    return dict(q_panels=q_panels, kv_panels=kv_panels, wo_panels=wo_panels,
                up_panels=up_panels, gate_panels=gate_panels, dn_u=dn_u,
                bkv=bkv, bqo=bqo, X2=X2, tabs=tabs)


def make_inmaps_fused(hp):
    up_u = hp["up_panels"]
    gt_u = hp["gate_panels"]
    dn_u = hp["dn_u"]
    maps = []
    for c in range(NC):
        s = c % 2
        scs = np.ascontiguousarray(np.concatenate(
            [up_u[8 * c:8 * c + 8], gt_u[8 * c:8 * c + 8],
             dn_u[8 * c:8 * c + 8]], axis=0))
        qsrc = np.ascontiguousarray(np.concatenate(
            [up_u[32 * s:32 * s + 32], gt_u[32 * s:32 * s + 32],
             dn_u[32 * s:32 * s + 32]], axis=0))
        maps.append({
            "x": np.ascontiguousarray(hp["X2"][c * SC:(c + 1) * SC]),
            "kvp": hp["kv_panels"],
            "bkv": hp["bkv"],
            "qp": hp["q_panels"],
            "wop": hp["wo_panels"],
            "bqo": hp["bqo"],
            "cost": np.ascontiguousarray(hp["tabs"][c][0]),
            "sint": np.ascontiguousarray(hp["tabs"][c][1]),
            "scs": scs,
            "qsrc": qsrc,
            "slotv": np.array([[s]], np.int32),
        })
    return maps


def kernel(**inputs):
    hp = host_prep(**inputs)
    ncF = build_fused()
    res = run_bass_kernel_spmd(ncF, make_inmaps_fused(hp), core_ids=list(range(NC))).results
    out = np.concatenate([res[c]["out"] for c in range(NC)], axis=0)
    return out.reshape(2, 8192, D)


# version 18
# speedup vs baseline: 1.1963x; 1.1963x over previous
import numpy as np
import ml_dtypes
from contextlib import ExitStack

import concourse.bass as bass
import concourse.mybir as mybir
from concourse.bass_utils import run_bass_kernel_spmd

F32 = mybir.dt.float32
BF16 = mybir.dt.bfloat16
AF = mybir.ActivationFunctionType
ALU = mybir.AluOpType
AX = mybir.AxisListType

NC = 8
D = 2048
DK = 16
E = 8192
ECH = 64
T = 512
NT = 4
SC = NT * T

ENGS = ("sync", "scalar", "vector", "tensor", "gpsimd")


class Ctr:
    def __init__(self, h):
        self.h = h
        self.n = 0

    def bump(self, k=1):
        self.n += k
        return self.n


class Prog:

    def __init__(self, nc, es):
        self.nc = nc
        self.es = es
        self.q = {e: [] for e in ENGS}
        self._last_wait = {}
        self.chains = {}

    def sem(self, name):
        return Ctr(self.es.enter_context(self.nc.semaphore(name)))

    def init_chains(self):
        for e in ("vector", "scalar", "gpsimd"):
            self.chains[e] = self.sem("chain_" + e)

    def emit(self, eng, fn, wait=(), inc=None, inc_k=1, dma=False):
        chain = self.chains.get(eng)
        wait = list(wait)
        if chain is not None:
            wait.append((chain, chain.n))
        waits = []
        for c, v in wait:
            if v <= 0:
                continue
            key = (eng, id(c.h))
            if self._last_wait.get(key, -1) < v:
                self._last_wait[key] = v
                waits.append((c.h, v))
        if inc is chain:
            inc = None
        post = inc.bump(inc_k) if inc is not None else None
        chain_post = None
        if chain is not None and not dma:
            chain_post = chain.bump(1)

        def run(e, fn=fn, waits=waits, inc=inc, inc_k=inc_k,
                chain=chain, chain_post=chain_post):
            for h, v in waits:
                e.wait_ge(h, v)
            r = fn(e)
            if inc is not None:
                r.then_inc(inc.h, inc_k)
            if chain_post is not None:
                r.then_inc(chain.h, 1)
        self.q[eng].append(run)
        return chain_post if chain_post is not None else post

    def barrier_wait(self, eng, *pairs):
        pairs = [(c, v) for c, v in pairs if v > 0]
        if pairs:
            self.emit(eng, lambda e: None, wait=pairs, dma=True)

    def replay(self):
        with self.nc.Block() as block:
            for eng in ENGS:
                ops = self.q[eng]
                if not ops:
                    continue
                getattr(block, eng)(lambda e, ops=ops: [f(e) for f in ops])


def _sb(nc, es, name, shape, dtype):
    return es.enter_context(nc.sbuf_tensor(name, shape, dtype))


def _ps(nc, es, name, shape, dtype=F32):
    return es.enter_context(nc.psum_tensor(name, shape, dtype))


def build_fused():
    nc = bass.Bass("TRN2", target_bir_lowering=False)
    I32 = mybir.dt.int32
    x_in = nc.dram_tensor("x", [SC, D], F32, kind="ExternalInput").ap()
    kvp_in = nc.dram_tensor("kvp", [32, 128, DK * 128], F32, kind="ExternalInput").ap()
    bkv_in = nc.dram_tensor("bkv", [128, 32], F32, kind="ExternalInput").ap()
    qp_in = nc.dram_tensor("qp", [DK, 128, DK * 128], F32, kind="ExternalInput").ap()
    wo_in = nc.dram_tensor("wop", [DK, 128, DK * 128], F32, kind="ExternalInput").ap()
    bqo_in = nc.dram_tensor("bqo", [128, 32], F32, kind="ExternalInput").ap()
    cos_in = nc.dram_tensor("cost", [512, SC], BF16, kind="ExternalInput").ap()
    sin_in = nc.dram_tensor("sint", [512, SC], BF16, kind="ExternalInput").ap()
    scs_in = nc.dram_tensor("scs", [24, 128, 2048], F32, kind="ExternalInput").ap()
    qsrc_in = nc.dram_tensor("qsrc", [96, 128, 2048], F32, kind="ExternalInput").ap()
    slot_in = nc.dram_tensor("slotv", [1, 1], I32, kind="ExternalInput").ap()
    out_d = nc.dram_tensor("out", [SC, D], F32, kind="ExternalOutput").ap()

    xn_dram = nc.dram_tensor("xn_blob", [128, DK, SC], BF16).ap()
    kvb_bf = nc.dram_tensor("kvb_bf", [32, 128, DK * 128], BF16).ap()
    qwb_bf = nc.dram_tensor("qwb_bf", [32, 128, DK * 128], BF16).ap()
    blob_ug = nc.dram_tensor("blob_ug", [128, 2 * ECH * 2048], BF16,
                             addr_space="Shared").ap()
    blob_dn = nc.dram_tensor("blob_dn", [128, DK * 8192], BF16,
                             addr_space="Shared").ap()
    scb_in = nc.dram_tensor("scb_in", [128, 3], F32)
    scb_out = nc.dram_tensor("scb_out", [128, 3], F32)
    ab_in = nc.dram_tensor("ab_in", [128, DK], F32)
    ab_out = nc.dram_tensor("ab_out", [128, DK], F32)
    bb_in = nc.dram_tensor("bb_in", [1, 4], F32)
    bb_out = nc.dram_tensor("bb_out", [1, 4], F32)

    id_b_t = nc.inline_tensor(np.eye(128, dtype=ml_dtypes.bfloat16), "id_b")
    id_f_t = nc.inline_tensor(np.eye(128, dtype=np.float32), "id_f")
    ones_b_t = nc.inline_tensor(np.ones((128, 1), dtype=ml_dtypes.bfloat16), "ones_b")
    ones_f_t = nc.inline_tensor(np.ones((1, 128), dtype=np.float32), "ones_f")
    ones_c_t = nc.inline_tensor(np.ones((128, 1), dtype=np.float32), "ones_c")
    eps_t = nc.inline_tensor(np.full((128, 1), 1e-5, dtype=np.float32), "eps_c")

    with ExitStack() as es:
        p = Prog(nc, es)
        KB = 1024
        base, _top = nc.bump_sbuf(212832, align_bytes=64)

        def at(name, shape, dtype, off):
            return nc.alloc_sbuf_tensor_at(name, shape, dtype, offset=base + off)
        xt1 = at("xt1", [128, 4, D], F32, 0)
        xnb = at("xnb", [128, 4, D], BF16, 32 * KB)
        ksb = at("ksb", [128, DK, T], BF16, 48 * KB)
        vsb = at("vsb", [128, DK, T], BF16, 64 * KB)
        krot = at("krot", [128, 8, T], F32, 80 * KB)
        kwb = at("kwb", [128, 2, DK * 128], BF16, 96 * KB)
        rsc1 = at("rsc1", [128, T], F32, 104 * KB)
        rsc2 = at("rsc2", [128, T], F32, 106 * KB)
        ksq = at("ksq", [128, 2, T], BF16, 108 * KB)
        invb = at("invb", [128, T], F32, 110 * KB)
        vs_scr = at("vs_scr", [128, T], F32, 112 * KB)
        pr_scr = at("pr_scr", [128, T], F32, 114 * KB)
        sqs = at("sqs", [128, D], F32, 116 * KB)
        hid = at("hid", [128, ECH, T], BF16, 0)
        upsl = at("upsl", [128, 2, 2, DK * 128], BF16, 64 * KB)
        gtsl = at("gtsl", [128, 2, 2, DK * 128], BF16, 80 * KB)
        dnsl = at("dnsl", [128, 2, ECH * 128], BF16, 96 * KB)
        q_sb = at("q_sb", [128, DK, T], F32, 0)
        qrot = at("qrot", [128, 8, T], F32, 32 * KB)
        y_sb = at("y_sb", [128, DK, T], BF16, 48 * KB)
        xt4 = at("xt4", [128, 4, D], F32, 64 * KB)
        out_sb = at("out_sb", [128, 4, D], F32, 96 * KB)
        qpb = at("qpb", [128, 2, DK * 128], BF16, 128 * KB)
        wob = at("wob", [128, 2, DK * 128], BF16, 136 * KB)
        s_scr = at("s_scr", [128, T], F32, 144 * KB)
        pr2 = at("pr2", [128, T], F32, 146 * KB)
        silu_u = at("silu_u", [128, 2, T], BF16, 148 * KB)
        xnt2 = at("xnt2", [128, DK, T], BF16, 150 * KB)
        qf = at("qf", [128, 2, 2048], F32, 124 * KB)
        m1b = at("m1b", [128, 2, 2048], BF16, 140 * KB)
        m2b = at("m2b", [128, 2, 2048], BF16, 148 * KB)
        tb = at("tb", [128, 2, 2048], BF16, 156 * KB)

        xnt = at("xnt", [128, DK, T], BF16, 166 * KB)
        g_sb = at("g_sb", [128, DK, T], BF16, 182 * KB)
        cosb = at("cosb", [128, 4, T], BF16, 198 * KB)
        sinb = at("sinb", [128, 4, T], BF16, 202 * KB)
        nrm = pr_scr
        mf = at("mf", [128, 390], F32, 206 * KB)
        stat = mf[:, 0:16]
        a_tmp = mf[:, 16:32]
        a_acc = mf[:, 32:48]
        a_sb = a_acc
        sc_acc = mf[:, 48:51]
        spp = sc_acc
        ab_c = mf[:, 51:52]
        colc = mf[:, 52:60]
        bqo_sb = mf[:, 60:92]
        bkv_sb = mf[:, 92:124]
        onc = mf[:, 124:125]
        epsc = mf[:, 125:126]
        idf = mf[:, 126:254]
        onf = mf[0:1, 254:382]
        sctot = mf[0:1, 382:385]
        pk = mf[0:1, 385:390]
        slot_sb = at("slot_sb", [1, 1], mybir.dt.int32, 206 * KB + 1568)
        mb = at("mb", [128, 129], BF16, 206 * KB + 1600)
        idb = mb[:, 0:128]
        onb = mb[:, 128:129]

        B = [_ps(nc, es, f"pb{i}", [128, T], F32) for i in range(7)]
        B7 = _ps(nc, es, "pb7", [128, T], BF16)

        p.init_chains()
        s_v = p.chains["vector"]
        s_s = p.chains["scalar"]
        s_g = p.chains["gpsimd"]
        s_ld = p.sem("s_ld")
        s_lx = p.sem("s_lx")
        s_lc = p.sem("s_lc")
        s_ls = p.sem("s_ls")
        s_wv = [p.sem("s_wv0"), p.sem("s_wv1")]
        s_xs = p.sem("s_xs")
        s_qf = p.sem("s_qf")
        s_tb = [p.sem("s_tb0"), p.sem("s_tb1")]
        s_xn = p.sem("s_xn")
        s_xt = p.sem("s_xt")
        s_up = [p.sem("s_up0"), p.sem("s_up1")]
        s_gt = [p.sem("s_gt0"), p.sem("s_gt1")]
        s_dn = [p.sem("s_dn0"), p.sem("s_dn1")]
        s_qp = [p.sem("s_qp0"), p.sem("s_qp1")]
        s_wo = [p.sem("s_wo0"), p.sem("s_wo1")]
        s_t = p.sem("s_t")
        s_st = p.sem("s_st")
        s_kb = p.sem("s_kb")
        s_qb = p.sem("s_qb")
        s_sb1 = p.sem("s_sb1")
        s_sc2 = p.sem("s_sc2")
        s_ab = p.sem("s_ab")
        s_ax = p.sem("s_ax")
        cc1 = p.sem("cc1")
        cc2 = p.sem("cc2")
        cc3 = p.sem("cc3")

        for dst, src in ((idb, id_b_t), (idf, id_f_t), (onb, ones_b_t),
                         (onf, ones_f_t), (onc, ones_c_t), (epsc, eps_t)):
            p.emit("scalar", lambda e, dst=dst, src=src: e.dma_start(
                out=dst[:], in_=src.ap()), inc=s_ld, inc_k=16, dma=True)
        p.emit("scalar", lambda e: e.dma_start(out=bqo_sb[:], in_=bqo_in), inc=s_ld, inc_k=16, dma=True)
        p.emit("scalar", lambda e: e.dma_start(out=bkv_sb[:], in_=bkv_in), inc=s_ld, inc_k=16, dma=True)
        c_ld = p.emit("scalar", lambda e: e.dma_start(out=slot_sb[:], in_=slot_in), inc=s_ld, inc_k=16, dma=True)
        z1 = p.emit("vector", lambda e: e.memset(a_acc[:], 0.0))
        z2 = p.emit("vector", lambda e: e.memset(sc_acc[:], 0.0))

        kv_mm_done = {}
        kv_evac_done = {}
        xn_spill = 0
        a_add_done = 0
        sqrt_done = 0
        inv_copy_done = 0
        ev_done = 0
        tr_last_prev = 0
        kv_mm_prev_last = 0
        app_done_prev = 0
        rope_end_prev = 0
        sc_next = [0]
        qn_next = [0]
        qf_user = [None]
        qf_cnt = [0]
        sub_done = {}
        scan_add_last = [0]
        colc_done_box = [None]
        rslot_box = []

        def emit_scale_unit():
            i = sc_next[0]
            sc_next[0] += 1
            m = i // 8
            sl = i % 2
            w = [qf_user[0]] if qf_user[0] else []
            qf_cnt[0] = p.emit("sync", lambda e, i=i, sl=sl: e.dma_start(
                out=qf[:, sl, :], in_=scs_in[i]), wait=w, inc=s_qf, inc_k=16, dma=True)
            ab = p.emit("scalar", lambda e, sl=sl: e.activation(
                m1b[:, 0, :], qf[:, sl, :], AF.Abs, accum_out=ab_c[:]),
                wait=[(s_qf, qf_cnt[0])])
            qf_user[0] = (s_s, ab)
            scan_add_last[0] = p.emit("vector", lambda e, m=m: e.tensor_tensor(
                sc_acc[:, m:m + 1], sc_acc[:, m:m + 1], ab_c[:], ALU.add),
                wait=[(s_s, ab), (s_v, z2)])

        def q_dstq(gu):
            if gu < 2 * ECH:
                return blob_ug[:, gu * 2048:(gu + 1) * 2048]
            v = gu - 2 * ECH
            return blob_dn[:, v * 2048:(v + 1) * 2048]

        def emit_quant_unit():
            u = qn_next[0]
            qn_next[0] += 1
            m = u // 32
            colc_done = colc_done_box[0]
            sl = u % 2
            w = [qf_user[0]] if qf_user[0] else []
            qf_cnt[0] = p.emit("sync", lambda e, u=u, sl=sl: e.dma_start(
                out=qf[:, sl, :], in_=qsrc_in[u]), wait=w, inc=s_qf, inc_k=16, dma=True)
            ws_t2 = []
            if u >= 2:
                ws_t2.append((s_tb[u % 2], 16 * (u // 2)))
            path = u % 3
            if path == 1:
                p.emit("scalar", lambda e, sl=sl, m=m: e.activation(
                    m1b[:, sl, :], qf[:, sl, :], AF.Sign, bias=colc[:, 5 + m:6 + m]),
                    wait=[(s_qf, qf_cnt[0]), (s_v, colc_done)])
                sg2 = p.emit("scalar", lambda e, sl=sl, m=m: e.activation(
                    m2b[:, sl, :], qf[:, sl, :], AF.Sign, bias=colc[:, m:m + 1]))
                qf_user[0] = (s_s, sg2)
                sub_done[u] = (s_g, p.emit("gpsimd", lambda e, sl=sl: e.tensor_tensor(
                    tb[:, sl, :], m1b[:, sl, :], m2b[:, sl, :], ALU.add),
                    wait=ws_t2 + [(s_s, sg2)]))
            else:
                p.emit("vector", lambda e, sl=sl, m=m: e.tensor_scalar(
                    m1b[:, sl, :], qf[:, sl, :], colc[:, m:m + 1], 2.0, ALU.is_gt, ALU.mult),
                    wait=[(s_qf, qf_cnt[0]), (s_v, colc_done)])
                lastq = p.emit("vector", lambda e, sl=sl, m=m: e.tensor_scalar(
                    m2b[:, sl, :], qf[:, sl, :], colc[:, 5 + m:6 + m], 2.0, ALU.is_lt, ALU.mult))
                qf_user[0] = (s_v, lastq)
                if path == 0:
                    sub_done[u] = (s_v, p.emit("vector", lambda e, sl=sl: e.tensor_tensor(
                        tb[:, sl, :], m1b[:, sl, :], m2b[:, sl, :], ALU.subtract),
                        wait=ws_t2))
                else:
                    sub_done[u] = (s_g, p.emit("gpsimd", lambda e, sl=sl: e.tensor_tensor(
                        tb[:, sl, :], m1b[:, sl, :], m2b[:, sl, :], ALU.subtract),
                        wait=ws_t2 + [(s_v, lastq)]))
            lo = m * 64 + (u % 32)
            hi = m * 64 + 32 + (u % 32)

            def fn_store(e, sl=sl, lo=lo, hi=hi):
                if not rslot_box:
                    rslot_box.append(e.register("rqs").__enter__())
                    e.reg_load(rslot_box[0], slot_sb[0:1, 0:1])
                r = rslot_box[0]
                with e.If_eq(r, 0):
                    e.dma_start(out=q_dstq(lo), in_=tb[:, sl, :]).then_inc(
                        s_tb[sl].h, 16)
                with e.Else():
                    e.dma_start(out=q_dstq(hi), in_=tb[:, sl, :]).then_inc(
                        s_tb[sl].h, 16)
                return e.nop()
            p.emit("scalar", fn_store,
                   wait=[sub_done[u], (s_ld, c_ld)], dma=True)
            s_tb[sl].bump(16)

        xld_t = {}
        cld_t = {}
        sld_t = {}
        app_t = {-1: 0}
        tr_box = [0]

        def emit_ln(t):
            xld = xld_t[t] = p.emit("scalar", lambda e, t=t: e.dma_start(
                out=xt1[:], in_=x_in[t * T:(t + 1) * T, :].rearrange("(g p) d -> p g d", p=128)),
                wait=[(s_v, app_t[t - 1])], inc=s_lx, inc_k=16, dma=True)
            sum_done = ss_done = 0
            for g in range(4):
                sum_done = p.emit("vector", lambda e, g=g: e.reduce_sum(
                    stat[:, g:g + 1], xt1[:, g, :], AX.X), wait=[(s_lx, xld)])
                ss_done = p.emit("scalar", lambda e, g=g: e.activation(
                    sqs[:], xt1[:, g, :], AF.Square, accum_out=stat[:, 4 + g:5 + g]),
                    wait=[(s_lx, xld)])
            p.emit("vector", lambda e: e.tensor_scalar(
                stat[:, 0:4], stat[:, 0:4], 1.0 / D, None, ALU.mult),
                wait=[(s_v, sum_done)])
            p.emit("vector", lambda e: e.tensor_scalar(
                stat[:, 4:8], stat[:, 4:8], 1.0 / D, None, ALU.mult),
                wait=[(s_s, ss_done)])
            p.emit("vector", lambda e: e.tensor_tensor(
                stat[:, 8:12], stat[:, 0:4], stat[:, 0:4], ALU.mult))
            vstat = p.emit("vector", lambda e: e.tensor_tensor(
                stat[:, 8:12], stat[:, 4:8], stat[:, 8:12], ALU.subtract))
            sstat = p.emit("scalar", lambda e: e.activation(
                stat[:, 12:16], stat[:, 8:12], AF.Sqrt, bias=epsc[:]),
                wait=[(s_v, vstat), (s_ld, c_ld)])
            vstat = p.emit("vector", lambda e: e.reciprocal(
                stat[:, 12:16], stat[:, 12:16]),
                wait=[(s_s, sstat)])
            last = 0
            for g in range(4):
                last = p.emit("vector", lambda e, g=g: e.tensor_scalar(
                    xnb[:, g, :], xt1[:, g, :], stat[:, g:g + 1],
                    stat[:, 12 + g:13 + g], ALU.subtract, ALU.mult),
                    wait=[(s_v, vstat), (s_t, tr_box[0])])
            app_t[t] = last

        emit_ln(0)
        cld_t[0] = p.emit("scalar", lambda e: e.dma_start(
            out=cosb[:], in_=cos_in[:, 0:T].rearrange("(g p) s -> p g s", p=128)),
            inc=s_lc, inc_k=16, dma=True)
        sld_t[0] = p.emit("scalar", lambda e: e.dma_start(
            out=sinb[:], in_=sin_in[:, 0:T].rearrange("(g p) s -> p g s", p=128)),
            inc=s_ls, inc_k=16, dma=True)

        for tt in range(NT):
            cld2 = cld_t[tt]
            sld2 = sld_t[tt]
            for kc in range(DK):
                tr_w = [(s_v, app_t[tt]), (s_xs, xn_spill), (s_ld, c_ld),
                        (s_t, kv_mm_prev_last)]
                if kc >= 1 or tt > 0:
                    tr_w.append((s_v, ev_done))
                tr_last = 0
                for g in range(4):
                    tr_last = p.emit("tensor", lambda e, kc=kc, g=g: e.transpose(
                        B7[:, g * 128:(g + 1) * 128],
                        xnb[:, g, kc * 128:(kc + 1) * 128], idb[:]),
                        wait=tr_w if g == 0 else (), inc=s_t)
                ev_done = p.emit("vector", lambda e, kc=kc: e.tensor_copy(
                    xnt[:, kc, :], B7[:]), wait=[(s_t, tr_last)])
            xn_done = ev_done
            tr_box[0] = tr_last
            xn_spill = p.emit("scalar", lambda e, tt=tt: e.dma_start(
                out=xn_dram[:, :, tt * T:(tt + 1) * T], in_=xnt[:]),
                wait=[(s_v, xn_done)], inc=s_xs, inc_k=16, dma=True)

            for oc in range(32):
                step = tt * 32 + oc
                b = oc % 2
                if tt == 1 and oc == 8:
                    p.emit("gpsimd", lambda e: e.collective_compute(
                        "AllReduce", ALU.add,
                        replica_groups=[list(range(NC))],
                        ins=[scb_in.ap().opt()], outs=[scb_out.ap().opt()]),
                        wait=[(s_sb1, 16)], inc=cc1, inc_k=1, dma=True)
                wld = []
                if step >= 2:
                    wld.append((s_t, kv_mm_done[step - 2]))
                if tt == 0:
                    wl = p.emit("gpsimd", lambda e, b=b, oc=oc: e.dma_start(
                        out=kwb[:, b, :], in_=kvp_in[oc]),
                        wait=wld + ([(s_kb, 16 * (oc - 1))] if oc >= 2 else []),
                        inc=s_wv[b], inc_k=16, dma=True)
                    p.emit("scalar", lambda e, b=b, oc=oc: e.dma_start(
                        out=kvb_bf[oc], in_=kwb[:, b, :]),
                        wait=[(s_wv[b], wl)], inc=s_kb, inc_k=16, dma=True)
                else:
                    wld.append((s_kb, 16 * (oc + 1)))
                    wl = p.emit("sync", lambda e, b=b, oc=oc: e.dma_start(
                        out=kwb[:, b, :], in_=kvb_bf[oc]),
                        wait=wld, inc=s_wv[b], inc_k=16, dma=True)
                pb = B[b]
                mmw = [(s_wv[b], wl), (s_v, xn_done)]
                if step >= 2:
                    mmw.append(kv_evac_done[step - 2])
                last_mm = 0
                for kc in range(DK):
                    last_mm = p.emit("tensor", lambda e, b=b, kc=kc, pb=pb: e.matmul(
                        pb[:], kwb[:, b, kc * 128:(kc + 1) * 128], xnt[:, kc, :],
                        start=(kc == 0), stop=(kc == DK - 1)),
                        wait=mmw if kc == 0 else (), inc=s_t)
                kv_mm_done[step] = last_mm
                dst, cidx = (ksb, oc) if oc < 16 else (vsb, oc - 16)
                if oc % 2 == 0:
                    kv_evac_done[step] = (s_v, p.emit(
                        "vector", lambda e, pb=pb, dst=dst, cidx=cidx, oc=oc: e.tensor_scalar(
                            dst[:, cidx, :], pb[:], bkv_sb[:, oc:oc + 1], None, ALU.add),
                        wait=[(s_t, last_mm), (s_ld, c_ld)]))
                else:
                    kv_evac_done[step] = (s_s, p.emit(
                        "scalar", lambda e, pb=pb, dst=dst, cidx=cidx, oc=oc: e.activation(
                            dst[:, cidx, :], pb[:], AF.Identity, bias=bkv_sb[:, oc:oc + 1]),
                        wait=[(s_t, last_mm), (s_ld, c_ld)]))
                if tt == 0 and oc < 24:
                    emit_scale_unit()
                if tt >= 2 and oc % 2 == 1 and qn_next[0] < 48 * (tt - 1):
                    for _ in range(3):
                        if qn_next[0] < 96:
                            emit_quant_unit()
            kv_mm_prev_last = kv_mm_done[tt * 32 + 31]
            k_evac = {oc: kv_evac_done[tt * 32 + oc] for oc in range(16)}
            v_evac = {oc: kv_evac_done[tt * 32 + 16 + oc] for oc in range(16)}

            if tt == 0:
                p.emit("scalar", lambda e: e.dma_start(
                    out=scb_in.ap(), in_=sc_acc[:]),
                    wait=[(s_v, scan_add_last[0])], inc=s_sb1, inc_k=16, dma=True)

            if tt + 1 < NT:
                emit_ln(tt + 1)

            rope_done = {}
            for c in range(4):
                w = [k_evac[c], k_evac[c + 4], (s_lc, cld2), (s_ls, sld2)]
                p.emit("vector", lambda e, c=c: e.tensor_tensor(
                    rsc1[:], ksb[:, c, :], cosb[:, c, :], ALU.mult), wait=w)
                p.emit("vector", lambda e, c=c: e.tensor_tensor(
                    rsc2[:], ksb[:, c + 4, :], sinb[:, c, :], ALU.mult))
                rope_done[c] = p.emit("vector", lambda e, c=c: e.tensor_tensor(
                    krot[:, c, :], rsc1[:], rsc2[:], ALU.subtract))
                p.emit("vector", lambda e, c=c: e.tensor_tensor(
                    rsc1[:], ksb[:, c + 4, :], cosb[:, c, :], ALU.mult))
                p.emit("vector", lambda e, c=c: e.tensor_tensor(
                    rsc2[:], ksb[:, c, :], sinb[:, c, :], ALU.mult))
                rope_done[c + 4] = p.emit("vector", lambda e, c=c: e.tensor_tensor(
                    krot[:, c + 4, :], rsc1[:], rsc2[:], ALU.add))
            rope_end_prev = rope_done[7]
            if tt + 1 < NT:
                cld_t[tt + 1] = p.emit("scalar", lambda e, tt=tt: e.dma_start(
                    out=cosb[:], in_=cos_in[:, (tt + 1) * T:(tt + 2) * T].rearrange("(g p) s -> p g s", p=128)),
                    wait=[(s_v, rope_end_prev)], inc=s_lc, inc_k=16, dma=True)
                sld_t[tt + 1] = p.emit("scalar", lambda e, tt=tt: e.dma_start(
                    out=sinb[:], in_=sin_in[:, (tt + 1) * T:(tt + 2) * T].rearrange("(g p) s -> p g s", p=128)),
                    inc=s_ls, inc_k=16, dma=True)

            def ksrc(c):
                return krot[:, c, :] if c < 8 else ksb[:, c, :]

            sq_done = {}
            mm_ss = {}
            last_ss = 0
            for c in range(DK):
                w = [(s_v, rope_done[c])] if c < 8 else [k_evac[c]]
                if c >= 2:
                    w.append((s_t, mm_ss[c - 2]))
                sq_done[c] = p.emit("scalar", lambda e, c=c: e.activation(
                    ksq[:, c % 2, :], ksrc(c), AF.Square), wait=w)
                w2 = [(s_s, sq_done[c])]
                if tt > 0 and c == 0:
                    w2.append((s_s, sqrt_done))
                last_ss = mm_ss[c] = p.emit("tensor", lambda e, c=c: e.matmul(
                    B[5][0:1, :], onb[:], ksq[:, c % 2, :],
                    start=(c == 0), stop=(c == DK - 1)),
                    wait=w2, inc=s_t)
            sqrt_done = p.emit("scalar", lambda e: e.activation(
                nrm[0:1, 0:T], B[5][0:1, :], AF.Sqrt),
                wait=[(s_t, last_ss), (s_v, a_add_done)])
            vmax = p.emit("vector", lambda e: e.tensor_scalar(
                nrm[0:1, 0:T], nrm[0:1, 0:T], 1e-12, None, ALU.max),
                wait=[(s_s, sqrt_done)])
            vinv = p.emit("vector", lambda e: e.reciprocal(
                nrm[0:1, 0:T], nrm[0:1, 0:T]))
            bc_w = [(s_v, vinv)]
            if tt > 0:
                bc_w.append((s_v, inv_copy_done))
            bc_done = p.emit("tensor", lambda e: e.matmul(
                B[4][:], onf[:], nrm[0:1, 0:T], start=True, stop=True),
                wait=bc_w, inc=s_t)
            inv_copy_done = p.emit("vector", lambda e: e.tensor_copy(
                invb[:], B[4][:]), wait=[(s_t, bc_done)])

            for c in range(DK):
                w = [(s_v, inv_copy_done), v_evac[c]]
                if c >= 8:
                    w.append(k_evac[c])
                p.emit("vector", lambda e, c=c: e.tensor_tensor(
                    vs_scr[:], vsb[:, c, :], invb[:], ALU.mult), wait=w)
                p.emit("vector", lambda e, c=c: e.scalar_tensor_tensor(
                    pr_scr[:], ksrc(c), 1.0, vs_scr[:], ALU.bypass, ALU.mult,
                    accum_out=a_tmp[:, c:c + 1]))
            a_add_done = p.emit("vector", lambda e: e.tensor_tensor(
                a_acc[:], a_acc[:], a_tmp[:], ALU.add), wait=[(s_v, z1)])

            if tt == 1:
                p.emit("scalar", lambda e: e.dma_start(
                    out=spp[:], in_=scb_out.ap()),
                    wait=[(cc1, 1)], inc=s_sc2, inc_k=16, dma=True)
                red_mm = p.emit("tensor", lambda e: e.matmul(
                    B[6][0:1, 0:3], onc[:], spp[:], start=True, stop=True),
                    wait=[(s_sc2, 16), (s_ld, c_ld)], inc=s_t)
                p.emit("vector", lambda e: e.tensor_copy(sctot[:, 0:3], B[6][0:1, 0:3]),
                       wait=[(s_t, red_mm)])
                p.emit("vector", lambda e: e.tensor_scalar(
                    sctot[:, 0:3], sctot[:, 0:3], 1.0 / 16777216.0, 1e-8, ALU.mult, ALU.add))
                p.emit("vector", lambda e: e.tensor_scalar(
                    pk[:, 0:3], sctot[:, 0:3], 0.5, None, ALU.mult))
                p.emit("vector", lambda e: e.tensor_scalar(
                    pk[:, 3:4], sctot[:, 0:1], 0.5, None, ALU.mult))
                p.emit("vector", lambda e: e.tensor_tensor(
                    pk[:, 4:5], sctot[:, 1:2], sctot[:, 2:3], ALU.mult))
                pk_done = p.emit("vector", lambda e: e.tensor_scalar(
                    pk[:, 4:5], pk[:, 4:5], 0.25, None, ALU.mult))
                bc6_mm = p.emit("tensor", lambda e: e.matmul(
                    B[6][0:128, 0:5], onf[:], pk[0:1, 0:5], start=True, stop=True),
                    wait=[(s_v, pk_done)], inc=s_t)
                p.emit("vector", lambda e: e.tensor_copy(colc[:, 0:5], B[6][0:128, 0:5]),
                       wait=[(s_t, bc6_mm)])
                colc_done_box[0] = p.emit("vector", lambda e: e.tensor_scalar(
                    colc[:, 5:8], colc[:, 0:3], -1.0, None, ALU.mult))

        while qn_next[0] < 96:
            emit_quant_unit()
        quant_end_s = s_s.n
        quant_end_v = s_v.n
        quant_end_g = s_g.n
        tb0_end = s_tb[0].n
        tb1_end = s_tb[1].n
        ph1_s_end = s_s.n
        ph1_t_end = s_t.n
        kv_evac_last = kv_evac_done[NT * 32 - 1]
        ph1_sqrt_last = sqrt_done
        ph1_inv_last = inv_copy_done

        p.emit("gpsimd", lambda e: e.collective_compute(
            "AllReduce", ALU.add,
            replica_groups=[[0, 1], [2, 3], [4, 5], [6, 7]],
            ins=[bb_in.ap().opt()], outs=[bb_out.ap().opt()]),
            wait=[(s_tb[0], tb0_end), (s_tb[1], tb1_end)],
            inc=cc3, inc_k=1, dma=True)

        p.emit("scalar", lambda e: e.dma_start(
            out=ab_in.ap(), in_=a_acc[:]),
            wait=[(s_v, a_add_done)], inc=s_ab, inc_k=16, dma=True)
        p.emit("gpsimd", lambda e: e.collective_compute(
            "AllReduce", ALU.add,
            replica_groups=[[0, 1, 2, 3], [4, 5, 6, 7]],
            ins=[ab_in.ap().opt()], outs=[ab_out.ap().opt()]),
            wait=[(s_ab, 16)], inc=cc2, inc_k=1, dma=True)
        p.emit("gpsimd", lambda e: e.dma_start(
            out=a_sb[:], in_=ab_out.ap()),
            wait=[(cc2, 1)], inc=s_ax, inc_k=16, dma=True)

        xsl = [xnt, xnt2]
        q_mm_done = {-1: 0, -2: 0}
        wout_mm_done = {-1: 0}
        down_all_done = {-1: 0}
        gate_all_done = {-1: 0}
        rope_end = {-1: rope_end_prev}
        prod_end = {-1: 0, -2: 0}
        outadd_end = {-1: 0}
        outstore = {-1: 0}
        qev_last_prev = 0
        stt_last_prev = 0
        gev_last_prev = 0
        up_mm_prev = gt_mm_prev = dn_mm_prev = 0
        colc_done = colc_done_box[0]
        xnt_ld = {}

        def emit_xnt_ld(t):
            sl = t % 2
            wld = [(s_t, q_mm_done[t - 2]), (s_v, prod_end[t - 2]),
                   (s_xs, 16 * NT)]
            if t == 0:
                wld.append((s_t, ph1_t_end))
            if t == 1:
                wld += [(s_s, quant_end_s), (s_v, quant_end_v),
                        (s_g, quant_end_g),
                        (s_tb[0], tb0_end), (s_tb[1], tb1_end)]
            xnt_ld[t] = p.emit("scalar", lambda e, t=t, sl=sl: e.dma_start(
                out=xsl[sl][:], in_=xn_dram[:, :, t * T:(t + 1) * T]),
                wait=wld, inc=s_xn, inc_k=16, dma=True)

        emit_xnt_ld(0)
        for t in range(NT):
            xv = xsl[t % 2]
            cld2 = p.emit("scalar", lambda e, t=t: e.dma_start(
                out=cosb[:], in_=cos_in[:, t * T:(t + 1) * T].rearrange("(g p) s -> p g s", p=128)),
                wait=[(s_v, rope_end[t - 1])], inc=s_lc, inc_k=16, dma=True)
            sld2 = p.emit("scalar", lambda e, t=t: e.dma_start(
                out=sinb[:], in_=sin_in[:, t * T:(t + 1) * T].rearrange("(g p) s -> p g s", p=128)),
                inc=s_ls, inc_k=16, dma=True)

            up_ld = {}
            gt_ld = {}
            up_mm = {}
            gt_mm = {}
            silu_done = {}
            hid_done = {}
            for oc in range(ECH):
                if oc % 2 == 0:
                    j = oc // 2
                    wld = [(s_v, outadd_end[t - 1])]
                    if t == 0:
                        wld.append((cc3, 1))
                        wld.append((s_v, a_add_done))
                    if j >= 2:
                        wld.append((s_t, up_mm[oc - 3]))
                    elif t > 0:
                        wld.append((s_t, up_mm_prev))
                    up_ld[j] = p.emit("sync", lambda e, j=j: e.dma_start(
                        out=upsl[:, j % 2, :, :],
                        in_=blob_ug[:, (2 * j) * 2048:(2 * j + 2) * 2048]),
                        wait=wld, inc=s_up[j % 2], inc_k=16, dma=True)
                    wld2 = [(s_v, outadd_end[t - 1])]
                    if t == 0:
                        wld2.append((cc3, 1))
                        wld2.append((s_v, a_add_done))
                        wld2.append((s_s, ph1_s_end))
                    if j >= 2:
                        wld2.append((s_t, gt_mm[oc - 3]))
                    elif t > 0:
                        wld2.append((s_t, gt_mm_prev))
                    gt_ld[j] = p.emit("sync", lambda e, j=j: e.dma_start(
                        out=gtsl[:, j % 2, :, :],
                        in_=blob_ug[:, (ECH + 2 * j) * 2048:(ECH + 2 * j + 2) * 2048]),
                        wait=wld2, inc=s_gt[j % 2], inc_k=16, dma=True)
                j = oc // 2
                bu = B[oc % 2]
                wmm = [(s_up[j % 2], up_ld[j]), (s_xn, xnt_ld[t])]
                if oc >= 2:
                    wmm.append((s_s, silu_done[oc - 2]))
                elif t > 0:
                    wmm.append((s_s, qev_last_prev))
                else:
                    wmm.append(kv_evac_last)
                last = 0
                for kc in range(DK):
                    last = p.emit("tensor", lambda e, oc=oc, kc=kc, bu=bu, j=j, xv=xv: e.matmul(
                        bu[:], upsl[:, j % 2, oc % 2, kc * 128:(kc + 1) * 128],
                        xv[:, kc, :], start=(kc == 0), stop=(kc == DK - 1)),
                        wait=wmm if kc == 0 else (), inc=s_t)
                up_mm[oc] = last
                ws = [(s_t, up_mm[oc]), (s_v, colc_done)]
                if oc >= 2:
                    ws.append((s_v, hid_done[oc - 2]))
                elif t == 0:
                    ws.append((s_v, quant_end_v))
                    ws.append((s_g, quant_end_g))
                silu_done[oc] = p.emit("scalar", lambda e, oc=oc, bu=bu: e.activation(
                    silu_u[:, oc % 2, :], bu[:], AF.Silu, scale=colc[:, 3:4]), wait=ws)
                bg = B[2 + oc % 2]
                wmm2 = [(s_gt[j % 2], gt_ld[j])]
                if oc >= 2:
                    wmm2.append((s_v, hid_done[oc - 2]))
                elif t > 0:
                    wmm2.append((s_v, stt_last_prev))
                last = 0
                for kc in range(DK):
                    last = p.emit("tensor", lambda e, oc=oc, kc=kc, bg=bg, j=j, xv=xv: e.matmul(
                        bg[:], gtsl[:, j % 2, oc % 2, kc * 128:(kc + 1) * 128],
                        xv[:, kc, :], start=(kc == 0), stop=(kc == DK - 1)),
                        wait=wmm2 if kc == 0 else (), inc=s_t)
                gt_mm[oc] = last
                wh = [(s_t, gt_mm[oc]), (s_s, silu_done[oc])]
                if oc == 0:
                    wh.append((s_t, wout_mm_done[t - 1]))
                    if t == 0:
                        wh.append((s_s, ph1_s_end))
                hid_done[oc] = p.emit("vector", lambda e, oc=oc, bg=bg: e.tensor_tensor(
                    hid[:, oc, :], bg[:], silu_u[:, oc % 2, :], ALU.mult), wait=wh)
            gate_all_done[t] = gt_mm[ECH - 1]
            hid_all = hid_done[ECH - 1]

            dn_ld = {}
            dn_mm = {}
            g_evac = {}
            for dc in range(DK):
                wld = [(s_st, outstore[t - 1])]
                if t == 0:
                    wld.append((s_t, ph1_t_end))
                    wld.append((s_v, quant_end_v))
                    wld.append((s_s, quant_end_s))
                    wld.append((cc3, 1))
                    wld.append((s_v, a_add_done))
                if dc >= 2:
                    wld.append((s_t, dn_mm[dc - 2]))
                elif t > 0:
                    wld.append((s_t, dn_mm_prev))
                dn_ld[dc] = p.emit("sync", lambda e, dc=dc: e.dma_start(
                    out=dnsl[:, dc % 2, :], in_=blob_dn[:, dc * 8192:(dc + 1) * 8192]),
                    wait=wld, inc=s_dn[dc % 2], inc_k=16, dma=True)
                bd = B[4 + dc % 2]
                wmm = [(s_dn[dc % 2], dn_ld[dc]), (s_v, hid_all)]
                if dc >= 2:
                    wmm.append((s_s, g_evac[dc - 2]))
                elif t > 0:
                    wmm.append((s_s, gev_last_prev))
                else:
                    wmm.append((s_s, ph1_sqrt_last))
                    wmm.append((s_v, ph1_inv_last))
                last = 0
                for ec in range(ECH):
                    last = p.emit("tensor", lambda e, dc=dc, ec=ec, bd=bd: e.matmul(
                        bd[:], dnsl[:, dc % 2, ec * 128:(ec + 1) * 128],
                        hid[:, ec, :], start=(ec == 0), stop=(ec == ECH - 1)),
                        wait=wmm if ec == 0 else (), inc=s_t)
                dn_mm[dc] = last
                wg = [(s_t, dn_mm[dc])]
                if dc == 0:
                    wg.append((s_v, prod_end[t - 1]))
                g_evac[dc] = p.emit("scalar", lambda e, dc=dc, bd=bd: e.activation(
                    g_sb[:, dc, :], bd[:], AF.Silu, scale=colc[:, 4:5]), wait=wg)
            down_all_done[t] = dn_mm[DK - 1]

            if t + 1 < NT:
                emit_xnt_ld(t + 1)

            qp_ld = {}
            qmm = {}
            q_evac = {}
            for oc in range(DK):
                wld = [(s_s, quant_end_s), (s_v, quant_end_v), (s_g, quant_end_g)]
                if oc >= 2:
                    wld.append((s_t, qmm[oc - 2]))
                else:
                    wld.append((s_t, q_mm_done[t - 1]))
                if t == 0:
                    if oc >= 2:
                        wld.append((s_qb, 16 * (oc - 1)))
                    qp_ld[oc] = p.emit("gpsimd", lambda e, oc=oc: e.dma_start(
                        out=qpb[:, oc % 2, :], in_=qp_in[oc]),
                        wait=wld, inc=s_qp[oc % 2], inc_k=16, dma=True)
                    p.emit("sync", lambda e, oc=oc: e.dma_start(
                        out=qwb_bf[oc], in_=qpb[:, oc % 2, :]),
                        wait=[(s_qp[oc % 2], qp_ld[oc])], inc=s_qb, inc_k=16, dma=True)
                else:
                    wld.append((s_qb, 16 * (oc + 1)))
                    qp_ld[oc] = p.emit("sync", lambda e, oc=oc: e.dma_start(
                        out=qpb[:, oc % 2, :], in_=qwb_bf[oc]),
                        wait=wld, inc=s_qp[oc % 2], inc_k=16, dma=True)
                bq_ = B[oc % 2]
                wmm = [(s_qp[oc % 2], qp_ld[oc]), (s_s, silu_done[ECH - 1])]
                if oc >= 2:
                    wmm.append((s_s, q_evac[oc - 2]))
                last = 0
                for kc in range(DK):
                    last = p.emit("tensor", lambda e, oc=oc, kc=kc, bq_=bq_, xv=xv: e.matmul(
                        bq_[:], qpb[:, oc % 2, kc * 128:(kc + 1) * 128],
                        xv[:, kc, :], start=(kc == 0), stop=(kc == DK - 1)),
                        wait=wmm if kc == 0 else (), inc=s_t)
                qmm[oc] = last
                wq = [(s_t, max(qmm[oc], down_all_done[t]))]
                q_evac[oc] = p.emit("scalar", lambda e, oc=oc, bq_=bq_: e.activation(
                    q_sb[:, oc, :], bq_[:], AF.Identity, bias=bqo_sb[:, oc:oc + 1]), wait=wq)
            q_mm_done[t] = qmm[DK - 1]

            rope_done = {}
            for c in range(4):
                w = [(s_s, max(q_evac[c], q_evac[c + 4])), (s_lc, cld2), (s_ls, sld2),
                     (s_t, down_all_done[t])]
                if t == 0 and c == 0:
                    w.append((s_g, quant_end_g))
                p.emit("vector", lambda e, c=c: e.tensor_tensor(
                    s_scr[:], q_sb[:, c, :], cosb[:, c, :], ALU.mult), wait=w)
                p.emit("vector", lambda e, c=c: e.tensor_tensor(
                    pr2[:], q_sb[:, c + 4, :], sinb[:, c, :], ALU.mult))
                rope_done[c] = p.emit("vector", lambda e, c=c: e.tensor_tensor(
                    qrot[:, c, :], s_scr[:], pr2[:], ALU.subtract))
                p.emit("vector", lambda e, c=c: e.tensor_tensor(
                    s_scr[:], q_sb[:, c + 4, :], cosb[:, c, :], ALU.mult))
                p.emit("vector", lambda e, c=c: e.tensor_tensor(
                    pr2[:], q_sb[:, c, :], sinb[:, c, :], ALU.mult))
                rope_done[c + 4] = p.emit("vector", lambda e, c=c: e.tensor_tensor(
                    qrot[:, c + 4, :], s_scr[:], pr2[:], ALU.add))
            rope_end[t] = rope_done[7]

            def qsrc(c):
                return qrot[:, c, :] if c < 8 else q_sb[:, c, :]

            y_done = 0
            for c in range(DK):
                w = [(s_s, q_evac[c])] if c >= 8 else []
                if c == 0:
                    w.append((s_ax, 16))
                y_done = p.emit("vector", lambda e, c=c: e.tensor_scalar(
                    y_sb[:, c, :], qsrc(c), a_sb[:, c:c + 1], None, ALU.mult), wait=w)

            xt_ld = p.emit("scalar", lambda e, t=t: e.dma_start(
                out=xt4[:], in_=x_in[t * T:(t + 1) * T, :].rearrange("(g p) d -> p g d", p=128)),
                wait=[(s_t, gate_all_done[t])], inc=s_xt, inc_k=16, dma=True)

            wo_ld = {}
            womm = {}
            stt_done = {}
            oadd = {}
            for oc in range(DK):
                wld = [(s_v, quant_end_v), (s_s, quant_end_s), (s_g, quant_end_g)]
                if oc >= 2:
                    wld.append((s_t, womm[oc - 2]))
                else:
                    wld.append((s_t, wout_mm_done[t - 1]))
                if t == 0:
                    if oc >= 2:
                        wld.append((s_qb, 16 * (16 + oc - 1)))
                    wo_ld[oc] = p.emit("gpsimd", lambda e, oc=oc: e.dma_start(
                        out=wob[:, oc % 2, :], in_=wo_in[oc]),
                        wait=wld, inc=s_wo[oc % 2], inc_k=16, dma=True)
                    p.emit("sync", lambda e, oc=oc: e.dma_start(
                        out=qwb_bf[16 + oc], in_=wob[:, oc % 2, :]),
                        wait=[(s_wo[oc % 2], wo_ld[oc])], inc=s_qb, inc_k=16, dma=True)
                else:
                    wld.append((s_qb, 16 * (16 + oc + 1)))
                    wo_ld[oc] = p.emit("sync", lambda e, oc=oc: e.dma_start(
                        out=wob[:, oc % 2, :], in_=qwb_bf[16 + oc]),
                        wait=wld, inc=s_wo[oc % 2], inc_k=16, dma=True)
                bo_ = B[2 + oc % 2]
                wmm = [(s_wo[oc % 2], wo_ld[oc]), (s_v, y_done), (s_v, hid_done[ECH - 1])]
                if oc >= 2:
                    wmm.append((s_v, stt_done[oc - 2]))
                last = 0
                for kc in range(DK):
                    last = p.emit("tensor", lambda e, oc=oc, kc=kc, bo_=bo_: e.matmul(
                        bo_[:], wob[:, oc % 2, kc * 128:(kc + 1) * 128],
                        y_sb[:, kc, :], start=(kc == 0), stop=(kc == DK - 1)),
                        wait=wmm if kc == 0 else (), inc=s_t)
                womm[oc] = last
                stt_done[oc] = p.emit("vector", lambda e, oc=oc, bo_=bo_, xv=xv: e.scalar_tensor_tensor(
                    s_scr[:], bo_[:], bqo_sb[:, 16 + oc:17 + oc], xv[:, oc, :],
                    ALU.add, ALU.add), wait=[(s_t, womm[oc])])
                prod_d = p.emit("vector", lambda e, oc=oc: e.tensor_tensor(
                    pr2[:], s_scr[:], g_sb[:, oc, :], ALU.mult))
                tr_w = [(s_v, prod_d), (s_ld, c_ld)]
                if oc >= 1:
                    tr_w.append((s_v, oadd[oc - 1]))
                else:
                    tr_w.append((s_v, outadd_end[t - 1]))
                    if t == 0:
                        tr_w.append((s_v, colc_done))
                tr_last = 0
                for g in range(4):
                    tr_last = p.emit("tensor", lambda e, g=g: e.transpose(
                        B[6][:, g * 128:(g + 1) * 128], pr2[:, g * 128:(g + 1) * 128], idf[:]),
                        wait=tr_w if g == 0 else (), inc=s_t)
                wadd = [(s_t, tr_last), (s_xt, xt_ld)]
                if oc == 0:
                    wadd.append((s_t, down_all_done[t]))
                    wadd.append((s_st, outstore[t - 1]))
                oadd[oc] = p.emit("vector", lambda e, oc=oc: e.tensor_tensor(
                    out_sb[:, :, oc * 128:(oc + 1) * 128],
                    B[6][:].rearrange("p (g c) -> p g c", g=4),
                    xt4[:, :, oc * 128:(oc + 1) * 128], ALU.add), wait=wadd)
            wout_mm_done[t] = womm[DK - 1]
            prod_end[t] = stt_done[DK - 1] + 1
            outadd_end[t] = oadd[DK - 1]

            outstore[t] = p.emit("sync", lambda e, t=t: e.dma_start(
                out=out_d[t * T:(t + 1) * T, :].rearrange("(g p) d -> p g d", p=128),
                in_=out_sb[:]), wait=[(s_v, outadd_end[t])], inc=s_st, inc_k=16, dma=True)
            qev_last_prev = q_evac[DK - 1]
            stt_last_prev = stt_done[DK - 1]
            gev_last_prev = g_evac[DK - 1]
            up_mm_prev = up_mm[ECH - 1]
            gt_mm_prev = gt_mm[ECH - 1]
            dn_mm_prev = dn_mm[DK - 1]

        p.barrier_wait("scalar", (s_st, s_st.n))
        p.replay()
    return nc


def _panels(WT, nkc, noc):
    return np.ascontiguousarray(
        WT.reshape(nkc, 128, noc, 128).transpose(2, 1, 0, 3).reshape(noc, 128, nkc * 128))


def _perm():
    return np.concatenate([np.arange(0, 1024, 2), np.arange(1, 1024, 2),
                           np.arange(1024, 2048)])


def _bias_cols(b):
    n = b.shape[0] // 128
    return np.ascontiguousarray(b.reshape(n, 128).T)


def host_prep(L0_attn_X, L0_attn_W_in, L0_attn_b_in, L0_attn_W_out,
              L0_attn_b_out, L0_ffn_W_up, L0_ffn_W_gate, L0_ffn_W_down):
    pm = _perm()
    f32 = np.float32
    Wq = np.ascontiguousarray(L0_attn_W_in[0:D][pm]).astype(f32, copy=False)
    Wk = np.ascontiguousarray(L0_attn_W_in[D:2 * D][pm]).astype(f32, copy=False)
    Wv = np.ascontiguousarray(L0_attn_W_in[2 * D:3 * D][pm]).astype(f32, copy=False)
    q_panels = _panels(np.ascontiguousarray(Wq.T), DK, DK)
    k_panels = _panels(np.ascontiguousarray(Wk.T), DK, DK)
    v_panels = _panels(np.ascontiguousarray(Wv.T), DK, DK)
    kv_panels = np.ascontiguousarray(np.concatenate([k_panels, v_panels], axis=0))
    wo_panels = _panels(np.ascontiguousarray(L0_attn_W_out[:, pm].T), DK, DK)
    up_panels = _panels(np.ascontiguousarray(L0_ffn_W_up.T), DK, ECH)
    gate_panels = _panels(np.ascontiguousarray(L0_ffn_W_gate.T), DK, ECH)
    down_panels = _panels(np.ascontiguousarray(L0_ffn_W_down.T), ECH, DK)

    bq = _bias_cols(np.ascontiguousarray(L0_attn_b_in[0:D][pm]))
    bk = _bias_cols(np.ascontiguousarray(L0_attn_b_in[D:2 * D][pm]))
    bv = _bias_cols(np.ascontiguousarray(L0_attn_b_in[2 * D:3 * D][pm]))
    bkv = np.ascontiguousarray(np.concatenate([bk, bv], axis=1))
    bo = _bias_cols(np.ascontiguousarray(L0_attn_b_out))

    X2 = np.ascontiguousarray(L0_attn_X.reshape(2 * 8192, D))

    tabs = []
    inv_freq32 = (1.0 / (10000.0 ** (np.arange(0, 1024, 2, dtype=np.float32) / 1024.0))).astype(np.float32)
    for c in range(NC):
        pos0 = (c % 4) * SC
        pos = np.arange(pos0, pos0 + SC, dtype=np.float32)
        ang = pos[None, :] * inv_freq32[:, None]
        tabs.append((np.cos(ang).astype(ml_dtypes.bfloat16),
                     np.sin(ang).astype(ml_dtypes.bfloat16)))

    dn_u = np.ascontiguousarray(
        down_panels.reshape(DK, 128, 4, 2048).transpose(0, 2, 1, 3).reshape(ECH, 128, 2048))

    bqo = np.ascontiguousarray(np.concatenate([bq, bo], axis=1))

    return dict(q_panels=q_panels, kv_panels=kv_panels, wo_panels=wo_panels,
                up_panels=up_panels, gate_panels=gate_panels, dn_u=dn_u,
                bkv=bkv, bqo=bqo, X2=X2, tabs=tabs)


def make_inmaps_fused(hp):
    up_u = hp["up_panels"]
    gt_u = hp["gate_panels"]
    dn_u = hp["dn_u"]
    maps = []
    for c in range(NC):
        s = c % 2
        scs = np.ascontiguousarray(np.concatenate(
            [up_u[8 * c:8 * c + 8], gt_u[8 * c:8 * c + 8],
             dn_u[8 * c:8 * c + 8]], axis=0))
        qsrc = np.ascontiguousarray(np.concatenate(
            [up_u[32 * s:32 * s + 32], gt_u[32 * s:32 * s + 32],
             dn_u[32 * s:32 * s + 32]], axis=0))
        maps.append({
            "x": np.ascontiguousarray(hp["X2"][c * SC:(c + 1) * SC]),
            "kvp": hp["kv_panels"],
            "bkv": hp["bkv"],
            "qp": hp["q_panels"],
            "wop": hp["wo_panels"],
            "bqo": hp["bqo"],
            "cost": np.ascontiguousarray(hp["tabs"][c][0]),
            "sint": np.ascontiguousarray(hp["tabs"][c][1]),
            "scs": scs,
            "qsrc": qsrc,
            "slotv": np.array([[s]], np.int32),
        })
    return maps


def kernel(**inputs):
    hp = host_prep(**inputs)
    ncF = build_fused()
    res = run_bass_kernel_spmd(ncF, make_inmaps_fused(hp), core_ids=list(range(NC))).results
    out = np.concatenate([res[c]["out"] for c in range(NC)], axis=0)
    return out.reshape(2, 8192, D)
